# version 02
# baseline (speedup 1.0000x reference)
import numpy as np
import ml_dtypes
from contextlib import ExitStack
import concourse.bass as bass
import concourse.mybir as mybir
from concourse.bass_utils import run_bass_kernel_spmd

F32 = mybir.dt.float32
BF16 = mybir.dt.bfloat16
AF = mybir.ActivationFunctionType
ALU = mybir.AluOpType
AX_ = mybir.AxisListType
BF = ml_dtypes.bfloat16

D = 1024
T = 2048
DC = D // 128
NTC = T // 512
NTT = T // 128
NCORES = 8
LN_EPS = 1e-5

ENGS = ["sync", "tensor", "vector", "scalar", "gpsimd"]
SEMS = ["dma", "dma0", "dma1", "dma2", "dma3", "xdma", "mm", "dve", "act", "gp"]
SEM_OF_ENG = {"tensor": "mm", "vector": "dve", "scalar": "act",
              "gpsimd": "gp"}
SCR_BYTES = 141 * 1024
PERSIST_BASE = 32768
EMIT_BASE = 36864


class Sched:

    def __init__(self, nc, es):
        self.nc = nc
        self.sem = {s: es.enter_context(nc.semaphore("s_" + s)) for s in SEMS}
        self.cnt = {s: 0 for s in SEMS}
        self.ops = {e: [] for e in ENGS}
        self.last_wait = {e: {s: 0 for s in SEMS} for e in ENGS}

    def add(self, eng, fn, waits=(), inc=None, inc_n=1):
        w = []
        for s, c in waits:
            if c > self.last_wait[eng][s]:
                w.append((s, c))
                self.last_wait[eng][s] = c
        self.ops[eng].append((fn, w, inc, inc_n))
        if inc is not None:
            self.cnt[inc] += inc_n
            return self.cnt[inc]
        return None

    def now(self, sem):
        return self.cnt[sem]

    def selfflush(self, eng):
        sem = SEM_OF_ENG[eng]
        c = self.cnt[sem]
        self.ops[eng].append((lambda e: None, [(sem, c)], None, 1))
        self.last_wait[eng][sem] = max(self.last_wait[eng][sem], c)

    def flush(self):
        nc = self.nc
        sem = self.sem
        ops = self.ops
        self.ops = {e: [] for e in ENGS}

        def emit(eng_obj, lst):
            for fn, waits, inc, inc_n in lst:
                for s, c in waits:
                    eng_obj.wait_ge(sem[s], c * 16 if s.startswith("dma") or s == "xdma"
                                    else c)
                instr = fn(eng_obj)
                if inc is not None:
                    assert instr is not None
                    instr.then_inc(sem[inc], inc_n * 16
                                   if (inc.startswith("dma") or inc == "xdma") else inc_n)

        with nc.Block() as blk:
            if ops["sync"]:
                @blk.sync
                def _(e):
                    emit(e, ops["sync"])
            if ops["tensor"]:
                @blk.tensor
                def _(e):
                    emit(e, ops["tensor"])
            if ops["vector"]:
                @blk.vector
                def _(e):
                    emit(e, ops["vector"])
            if ops["scalar"]:
                @blk.scalar
                def _(e):
                    emit(e, ops["scalar"])
            if ops["gpsimd"]:
                @blk.gpsimd
                def _(e):
                    emit(e, ops["gpsimd"])


class Carver:

    def __init__(self, ctx, base, limit=None):
        self.ctx = ctx
        self.off = base
        self.base = base
        self.limit = limit if limit is not None else SCR_BYTES

    def fork(self, offset, limit=None):
        return Carver(self.ctx, offset, limit)

    def take(self, nbytes, align=64):
        o = (self.off + align - 1) // align * align
        self.off = o + nbytes
        assert self.off <= self.limit, f"SBUF scratch overflow: {self.off}"
        return o

    def f32(self, cols, parts=128, p0=0):
        o = self.take(cols * 4)
        return self.ctx.SCRF[p0:p0 + parts, o // 4:o // 4 + cols]

    def bf16(self, cols, parts=128, p0=0):
        o = self.take(cols * 2)
        return self.ctx.SCRB[p0:p0 + parts, o // 2:o // 2 + cols]


class Ctx:
    def __init__(self, nc, es):
        self.nc = nc
        self.es = es
        self.S = Sched(nc, es)
        self.AX = es.enter_context(
            nc.sbuf_tensor("AXa", [128, DC * T], F32))[:, :]
        self.SCRF = es.enter_context(
            nc.sbuf_tensor("SCRa", [128, SCR_BYTES // 4], F32))[:, :]
        self.SCRB = self.SCRF.bitcast(BF16)
        self.AXN = self.SCRB[:, 0:DC * T]
        self.pb = [es.enter_context(nc.psum_tensor(f"pb{i}", [128, 512], F32))
                   for i in range(8)]
        self.pbB = [p.bitcast(BF16) for p in self.pb]

    def carver(self):
        return Carver(self, EMIT_BASE)

    def xc(self, dc, tc=None):
        if tc is None:
            return self.AX[:, dc * T:(dc + 1) * T]
        return self.AX[:, dc * T + tc * 512: dc * T + (tc + 1) * 512]

    def xnc(self, dc, tc=None):
        if tc is None:
            return self.AXN[:, dc * T:(dc + 1) * T]
        return self.AXN[:, dc * T + tc * 512: dc * T + (tc + 1) * 512]



CONST_COLS_B = {}
CONST_COLS_F = {}
_CSTB_ARR = None
_CSTF_ARR = None


def _init_consts():
    global _CSTB_ARR, _CSTF_ARR
    if _CSTB_ARR is not None:
        return
    cb, cf = [], []

    def addb(name, arr):
        c0 = sum(a.shape[1] for a in cb)
        CONST_COLS_B[name] = (c0, arr.shape[1])
        cb.append(arr)

    def addf(name, arr):
        c0 = sum(a.shape[1] for a in cf)
        CONST_COLS_F[name] = (c0, arr.shape[1])
        cf.append(arr)

    ident = np.eye(128, dtype=np.float32)
    addb("ident", ident)
    sel = np.zeros((128, 8 * 128), np.float32)
    for e in range(8):
        sel[e, e * 128:(e + 1) * 128] = 1.0
    addb("sel8", sel)
    ones_row = np.zeros((128, 128), np.float32)
    ones_row[0, :] = 1.0
    addb("ones_row", ones_row)
    addb("ones_col", np.ones((128, 1), np.float32))
    addf("ones_row", ones_row)
    ones4 = np.zeros((128, 128), np.float32)
    ones4[[0, 32, 64, 96], :] = 1.0
    addf("ones4", ones4)

    _CSTB_ARR = np.concatenate(cb, axis=1).astype(BF)
    _CSTF_ARR = np.concatenate(cf, axis=1).astype(np.float32)


def cstb(ctx, name, parts=128):
    c0, n = CONST_COLS_B[name]
    return ctx.CSTB[0:parts, c0:c0 + n]


def cstf(ctx, name, parts=128):
    c0, n = CONST_COLS_F[name]
    return ctx.CSTF[0:parts, c0:c0 + n]



def emit_load_globals(ctx, dram):
    S = ctx.S
    cv = Carver(ctx, PERSIST_BASE)
    ctx.CSTB = cv.bf16(_CSTB_ARR.shape[1])
    ctx.CSTF = cv.f32(_CSTF_ARR.shape[1])
    for dc in range(DC):
        S.add("sync", lambda e, dc=dc: e.dma_start(
            ctx.xc(dc), dram["xin"][dc * 128:(dc + 1) * 128, :]),
            inc="xdma")
    S.add("sync", lambda e: e.dma_start(ctx.CSTB, dram["cst_b16"][:, :]),
          inc="dma")
    S.add("sync", lambda e: e.dma_start(ctx.CSTF, dram["cst_f32"][:, :]),
          inc="dma")
    ctx.globals_dma = S.now("dma")
    S.add("sync", lambda e: None, waits=[("dma", ctx.globals_dma)])
    ctx.x_ready = ("xdma", S.now("xdma"))


def emit_store_spine(ctx, dram):
    S = ctx.S
    waits = [(s, S.now(s)) for s in ["mm", "dve", "act", "gp"]]
    for dc in range(DC):
        S.add("sync", lambda e, dc=dc: e.dma_start(
            dram["xout"][dc * 128:(dc + 1) * 128, :], ctx.xc(dc)),
            waits=waits, inc="xdma")
    S.add("sync", lambda e: None, waits=[("xdma", S.now("xdma")),
                                         ("dma", S.now("dma")),
                                         ("dma0", S.now("dma0")),
                                         ("dma1", S.now("dma1"))])


def make_ln_scratch(cv):
    rows = cv.f32(2 * T)
    return {"rows": rows, "xsq": cv.bf16(T), "tmp": cv.f32(512)}


def emit_spine_ln(ctx, cv, src=None, dst=None, gcol=None, bcol=None,
                  scr=None):
    S = ctx.S
    src = ctx.AX if src is None else src
    dst = ctx.AXN if dst is None else dst
    scr = make_ln_scratch(cv) if scr is None else scr
    rows = scr["rows"]
    mrow = rows[0:1, 0:T]
    rrow = rows[32:33, 0:T]
    wrow = rows[32:33, T:2 * T]
    xsq = scr["xsq"]
    tmp = scr["tmp"]
    src_bf = src.dtype == BF16

    entry = [("dve", S.now("dve")), ("gp", S.now("gp")),
             ("mm", S.now("mm")), ("act", S.now("act")),
             ("dma", S.now("dma"))]
    if getattr(ctx, "x_ready", None):
        entry.append(ctx.x_ready)

    mm_done = [0] * DC
    act_done = [0] * DC
    for dc in range(DC):
        w = list(entry) if dc == 0 else [("mm", mm_done[dc - 1])]
        xb_dc = (src if src_bf else dst)[:, dc * T:(dc + 1) * T]
        if not src_bf:
            S.add("scalar", lambda e, dc=dc, xb_dc=xb_dc: e.activation(
                xb_dc, src[:, dc * T:(dc + 1) * T], AF.Copy), waits=w)
            w = []
        act_done[dc] = S.add("scalar", lambda e, dc=dc: e.activation(
            xsq, src[:, dc * T:(dc + 1) * T], AF.Square), waits=w,
            inc="act")
        for tc in range(NTC):
            w = [("act", act_done[dc])]
            if dc == 0:
                w = w + entry
            S.add("tensor", lambda e, dc=dc, tc=tc, xb_dc=xb_dc: e.matmul(
                ctx.pb[tc][0:1, :], cstb(ctx, "ones_col"),
                xb_dc[:, tc * 512:(tc + 1) * 512],
                start=(dc == 0), stop=(dc == DC - 1)), waits=w)
            mm_done[dc] = S.add("tensor", lambda e, dc=dc, tc=tc: e.matmul(
                ctx.pb[4 + tc][0:1, :], cstb(ctx, "ones_col"),
                xsq[:, tc * 512:(tc + 1) * 512],
                start=(dc == 0), stop=(dc == DC - 1)), inc="mm")
    stats_mm = S.now("mm")

    for tc in range(NTC):
        sl5 = slice(tc * 512, (tc + 1) * 512)
        S.add("vector", lambda e, tc=tc, sl5=sl5: e.tensor_scalar(
            mrow[:, sl5], ctx.pb[tc][0:1, :], 1.0 / D, None, ALU.mult),
            waits=[("mm", stats_mm)])
        S.add("vector", lambda e, tc=tc, sl5=sl5: e.tensor_scalar(
            rrow[:, sl5], ctx.pb[4 + tc][0:1, :], 1.0 / D, None, ALU.mult))
    S.add("vector", lambda e: e.tensor_tensor(wrow, mrow, mrow, ALU.mult))
    ev = S.add("vector", lambda e: e.scalar_tensor_tensor(
        rrow, rrow, LN_EPS, wrow, ALU.add, ALU.subtract), inc="dve")
    sq = S.add("scalar", lambda e: e.activation(wrow, rrow, AF.Sqrt),
               waits=[("dve", ev)], inc="act")
    rc = S.add("vector", lambda e: e.reciprocal(rrow, wrow),
               waits=[("act", sq)], inc="dve")

    o4 = cstf(ctx, "ones4")
    bc = 0
    for tc in range(NTC):
        sl5 = slice(tc * 512, (tc + 1) * 512)
        S.add("tensor", lambda e, sl5=sl5, tc=tc: e.matmul(
            ctx.pb[tc][:, :], o4[0:1, :],
            mrow[:, sl5], start=True, stop=True), waits=[("dve", rc)])
        bc = S.add("tensor", lambda e, sl5=sl5, tc=tc: e.matmul(
            ctx.pb[4 + tc][:, :], o4[32:33, :],
            rrow[:, sl5], start=True, stop=True), inc="mm")

    last = 0
    for dc in range(DC):
        for tc in range(NTC):
            s = src[:, dc * T + tc * 512: dc * T + (tc + 1) * 512]
            d = dst[:, dc * T + tc * 512: dc * T + (tc + 1) * 512]
            pm = ctx.pb[tc][:, :]
            pr = ctx.pb[4 + tc][:, :]
            S.add("vector", lambda e, s=s, pm=pm: e.tensor_tensor(
                tmp, s, pm, ALU.subtract), waits=[("mm", bc)])
            if gcol is None:
                last = S.add("vector", lambda e, d=d, pr=pr: e.tensor_tensor(
                    d, tmp, pr, ALU.mult), inc="dve")
            else:
                S.add("vector", lambda e, pr=pr: e.tensor_tensor(
                    tmp, tmp, pr, ALU.mult))
                last = S.add("vector", lambda e, d=d, dc=dc: e.tensor_scalar(
                    d, tmp, gcol[:, dc:dc + 1], bcol[:, dc:dc + 1],
                    ALU.mult, ALU.add), inc="dve")
    return last


def emit_f0(ctx, cv, L, xn_ready, coef, dbg=None):
    S = ctx.S
    E = 8
    stg = [cv.f32(1024) for _ in range(2)]
    w1h = cv.bf16(8 * 1024)
    w2h = cv.bf16(8 * 1024)
    h = cv.bf16(8 * 512)
    wtm = cv.bf16(NTT * 8)
    wT8 = cv.bf16(T, parts=8)
    wbc = cv.bf16(NTC * 512)
    b2b = cv.bf16(1024, parts=8)
    brb = cv.bf16(8, parts=1)
    wrb = cv.bf16(64)
    t1 = [cv.f32(512) for _ in range(2)]
    lsc = cv.f32(NTT * 8)
    esc = cv.f32(NTT * 8)
    srw = cv.f32(3 * NTT)
    b1f = cv.f32(128)
    smf = cv.f32(1104)

    S.add("sync", lambda e: e.dma_start(smf[:, 0:64], L["p_Wr"][:, :]),
          inc="dma")
    dWr = S.now("dma")
    S.add("sync", lambda e: e.dma_start(b1f, L["p_b1fm"][:, :]), inc="dma")
    dB1 = S.now("dma")
    S.add("sync", lambda e: e.dma_start(smf[0:8, 80:1104], L["b2"][:, :]),
          inc="dma")
    dB2 = S.now("dma")
    S.add("sync", lambda e: e.dma_start(smf[0:1, 64:72], L["br"][:, :]),
          inc="dma")
    dBr = S.now("dma")
    S.add("sync", lambda e: None, waits=[("dma", dBr)])
    S.add("vector", lambda e: e.tensor_copy(wrb, smf[:, 0:64]),
          waits=[("dma", dBr)])
    S.add("vector", lambda e: e.tensor_copy(b2b, smf[0:8, 80:1104]))
    smalls_dve = S.add("vector", lambda e: e.tensor_copy(
        brb, smf[0:1, 64:72]), inc="dve")

    mm0 = S.now("mm")
    mm_router_end = mm0 + NTT
    mm_c = mm_router_end + 2
    mm_wbc_end = {}
    mm_phase_end = {}
    for e in range(E):
        mm_c += NTC
        mm_wbc_end[e] = mm_c
        for hh in range(2):
            mm_c += NTC * 16
            mm_phase_end[(e, hh)] = mm_c
    mm_bias_end = mm_c + DC * NTC

    def prev_phase_end(e, hh):
        if (e, hh) == (0, 0):
            return None
        pe, ph = (e, 0) if hh == 1 else (e - 1, 1)
        return mm_phase_end[(pe, ph)]

    gp0 = S.now("gp")
    cast_done = {}
    g = 0
    stream_phases = []
    if dbg is None:
        stream_phases = [(e, hh) for e in range(E) for hh in range(2)]
    elif dbg == "h0":
        stream_phases = [(0, 0)]
    if dbg == "lgt":
        stream_phases = []
    for e, hh in stream_phases:
        if True:
            freq = prev_phase_end(e, hh) if dbg is None else None
            for k in range(16):
                par = g % 2
                psem = "dma%d" % par
                if k < 8:
                    src = L["W1"][e, k * 128:(k + 1) * 128,
                                  hh * 1024:(hh + 1) * 1024]
                    dst = w1h[:, k * 1024:(k + 1) * 1024]
                else:
                    src = L["W2"][e, hh * 1024 + (k - 8) * 128:
                                  hh * 1024 + (k - 7) * 128, :]
                    dst = w2h[:, (k - 8) * 1024:(k - 7) * 1024]
                w = [("gp", gp0 + g - 1)] if g >= 2 else []
                dcnt = S.add("sync", lambda e2, src=src, par=par:
                             e2.dma_start(stg[par], src), waits=w, inc=psem)
                w = [(psem, dcnt)]
                if k == 0 and freq is not None:
                    w.append(("mm", freq))
                c = S.add("gpsimd", lambda e2, dst=dst, par=par:
                          e2.tensor_copy(dst, stg[par]), waits=w, inc="gp")
                g += 1
            cast_done[(e, hh)] = c

    for tt in range(NTT):
        psl = ctx.pb[tt // 4][:, (tt % 4) * 8:(tt % 4) * 8 + 8]
        for dc in range(DC):
            S.add("tensor", lambda e, tt=tt, dc=dc, psl=psl: e.matmul(
                psl, ctx.AXN[:, dc * T + tt * 128: dc * T + (tt + 1) * 128],
                wrb[:, dc * 8:(dc + 1) * 8], start=(dc == 0), stop=False),
                waits=[("dve", max(xn_ready, smalls_dve))])
        S.add("tensor", lambda e, psl=psl: e.matmul(
            psl, cstb(ctx, "ones_row", 1), brb,
            start=False, stop=True), inc="mm")
    assert S.now("mm") == mm_router_end

    if dbg == "lgt":
        for tt in range(NTT):
            S.add("scalar", lambda e, tt=tt: e.activation(
                ctx.AX[:, tt * 8:(tt + 1) * 8],
                ctx.pb[tt // 4][:, (tt % 4) * 8:(tt % 4) * 8 + 8], AF.Copy),
                waits=[("mm", mm0 + (tt // 4 + 1) * 4)], inc="act")
        return
    rm = srw[:, 2 * NTT:3 * NTT]
    sume = srw[:, 0:NTT]
    pst = srw[:, NTT:2 * NTT]
    v_r, a_r = [], []
    for tt in range(NTT):
        psl = ctx.pb[tt // 4][:, (tt % 4) * 8:(tt % 4) * 8 + 8]
        S.add("vector", lambda e, tt=tt, psl=psl: e.tensor_reduce(
            rm[:, tt:tt + 1], psl, AX_.X, ALU.max),
            waits=[("mm", mm0 + (tt // 4 + 1) * 4)])
    S.add("vector", lambda e: e.tensor_copy(rm[:, 0:1], rm[:, 0:1]),
          inc="dve")
    S.selfflush("vector")
    for tt in range(NTT):
        psl = ctx.pb[tt // 4][:, (tt % 4) * 8:(tt % 4) * 8 + 8]
        v_r.append(S.add("vector", lambda e, tt=tt, psl=psl: e.tensor_scalar(
            lsc[:, tt * 8:(tt + 1) * 8], psl, rm[:, tt:tt + 1], None,
            ALU.subtract), inc="dve"))
    for tt in range(NTT):
        a_r.append(S.add("scalar", lambda e, tt=tt: e.activation(
            esc[:, tt * 8:(tt + 1) * 8], lsc[:, tt * 8:(tt + 1) * 8],
            AF.Exp), waits=[("dve", v_r[tt])], inc="act"))
    for tt in range(NTT):
        S.add("vector", lambda e, tt=tt: e.tensor_reduce(
            sume[:, tt:tt + 1], esc[:, tt * 8:(tt + 1) * 8], AX_.X, ALU.add),
            waits=[("act", a_r[tt])])
    for tt in range(NTT):
        S.add("vector", lambda e, tt=tt: e.reciprocal(
            pst[:, tt:tt + 1], sume[:, tt:tt + 1]), inc="dve")
    S.selfflush("vector")
    for tt in range(NTT):
        S.add("vector", lambda e, tt=tt: e.tensor_scalar(
            wtm[:, tt * 8:(tt + 1) * 8],
            ctx.pb[tt // 4][:, (tt % 4) * 8:(tt % 4) * 8 + 8],
            rm[:, tt:tt + 1], pst[:, tt:tt + 1], ALU.is_equal, ALU.mult))
    wtm_done = S.add("vector", lambda e: e.tensor_copy(
        esc[:, 0:8], wtm[:, 0:8]), inc="dve")
    if dbg == "wtm":
        S.add("scalar", lambda e: e.activation(
            ctx.AX[:, 0:NTT * 8], wtm, AF.Copy),
            waits=[("dve", wtm_done)], inc="act")
        return

    tr_dve = []
    for r in range(2):
        for k in range(8):
            tt = r * 8 + k
            w = [("dve", wtm_done)] if r == 0 else [("dve", tr_dve[0])]
            S.add("tensor", lambda e, tt=tt, k=k: e.transpose(
                ctx.pbB[7][0:8, k * 128:(k + 1) * 128],
                wtm[:, tt * 8:(tt + 1) * 8], cstb(ctx, "ident")),
                waits=(w if k == 0 else []))
        S.add("tensor", lambda e: e.matmul(
            ctx.pb[6][0:1, 0:8], cstb(ctx, "ones_col"), wtm[:, 0:8],
            start=True, stop=True), inc="mm")
        tr_dve.append(S.add("vector", lambda e, r=r: e.tensor_copy(
            wT8[0:8, r * 1024:(r + 1) * 1024], ctx.pbB[7][0:8, 0:1024]),
            waits=[("mm", S.now("mm"))], inc="dve"))
    wT8_done = tr_dve[-1]
    if dbg != "wT8":
        assert S.now("mm") == mm_router_end + 2
    if dbg == "wT8":
        S.add("scalar", lambda e: e.activation(
            ctx.AX[0:8, 0:T], wT8, AF.Copy),
            waits=[("dve", wT8_done)], inc="act")
        return

    bank_free = {i: ("dve", wtm_done) for i in range(4)}
    wbc_free = {4: ("dve", 0), 5: ("dve", 0)}

    for e in range(E):
        for tc in range(NTC):
            bk = 4 + tc % 2
            mmc = S.add("tensor", lambda e2, e=e, tc=tc, bk=bk: e2.matmul(
                ctx.pb[bk][:, :],
                cstb(ctx, "sel8", 8)[:, e * 128:(e + 1) * 128],
                wT8[0:8, tc * 512:(tc + 1) * 512], start=True, stop=True),
                waits=[("dve", wT8_done), wbc_free[bk]], inc="mm")
            c = S.add("vector", lambda e2, tc=tc, bk=bk: e2.tensor_scalar(
                wbc[:, tc * 512:(tc + 1) * 512], ctx.pb[bk][:, :],
                coef, None, ALU.mult), waits=[("mm", mmc)], inc="dve")
            wbc_free[bk] = ("dve", c)
        assert S.now("mm") == mm_wbc_end[e]

        for hh in range(2):
            cd = cast_done[(e, hh)]
            for tc in range(NTC):
                for hc in range(8):
                    bk = hc % 4
                    w = [("gp", cd), bank_free[bk]]
                    mmc = 0
                    for dc in range(DC):
                        mmc = S.add(
                            "tensor",
                            lambda e2, hc=hc, tc=tc, dc=dc, bk=bk: e2.matmul(
                                ctx.pb[bk][:, :],
                                w1h[:, dc * 1024 + hc * 128:
                                    dc * 1024 + (hc + 1) * 128],
                                ctx.AXN[:, dc * T + tc * 512:
                                        dc * T + (tc + 1) * 512],
                                start=(dc == 0), stop=(dc == DC - 1)),
                            waits=(w if dc == 0 else []),
                            inc=("mm" if dc == DC - 1 else None))
                    bi = e * 16 + hh * 8 + hc
                    c = S.add("vector",
                              lambda e2, hc=hc, bk=bk, bi=bi: e2.tensor_scalar(
                                  h[:, hc * 512:(hc + 1) * 512],
                                  ctx.pb[bk][:, :], b1f[:, bi:bi + 1], 0.0,
                                  ALU.add, ALU.max),
                              waits=[("mm", mmc)], inc="dve")
                    bank_free[bk] = ("dve", c)
                h_ready = S.now("dve")
                if dbg == "h0" and (e, hh, tc) == (0, 0, 0):
                    for hc in range(8):
                        S.add("scalar", lambda e2, hc=hc: e2.activation(
                            ctx.AX[:, hc * 512:(hc + 1) * 512],
                            h[:, hc * 512:(hc + 1) * 512], AF.Copy),
                            waits=[("dve", h_ready)], inc="act")
                    return
                for dcy in range(DC):
                    bk = dcy % 4
                    w = [("dve", h_ready), bank_free[bk]]
                    mmc = 0
                    for hc in range(8):
                        mmc = S.add(
                            "tensor",
                            lambda e2, dcy=dcy, hc=hc, bk=bk: e2.matmul(
                                ctx.pb[bk][:, :],
                                w2h[:, hc * 1024 + dcy * 128:
                                    hc * 1024 + (dcy + 1) * 128],
                                h[:, hc * 512:(hc + 1) * 512],
                                start=(hc == 0), stop=(hc == 7)),
                            waits=(w if hc == 0 else []),
                            inc=("mm" if hc == 7 else None))
                    c = S.add("vector",
                              lambda e2, bk=bk, tc=tc:
                              e2.tensor_tensor(
                                  t1[0], ctx.pb[bk][:, :],
                                  wbc[:, tc * 512:(tc + 1) * 512], ALU.mult),
                              waits=[("mm", mmc)], inc="dve")
                    bank_free[bk] = ("dve", c)
                    S.add("vector",
                          lambda e2, dcy=dcy, tc=tc:
                          e2.tensor_tensor(
                              ctx.xc(dcy, tc), ctx.xc(dcy, tc), t1[0],
                              ALU.add))
            assert S.now("mm") == mm_phase_end[(e, hh)], \
                (e, hh, S.now("mm"), mm_phase_end[(e, hh)])

    for dc in range(DC):
        for tc in range(NTC):
            bk = (dc * NTC + tc) % 4
            mmc = S.add("tensor", lambda e2, dc=dc, tc=tc, bk=bk: e2.matmul(
                ctx.pb[bk][:, :], b2b[0:8, dc * 128:(dc + 1) * 128],
                wT8[0:8, tc * 512:(tc + 1) * 512], start=True, stop=True),
                waits=[bank_free[bk]], inc="mm")
            c = S.add("vector", lambda e2, dc=dc, tc=tc, bk=bk:
                      e2.scalar_tensor_tensor(
                          ctx.xc(dc, tc), ctx.pb[bk][:, :], coef,
                          ctx.xc(dc, tc), ALU.mult, ALU.add),
                      waits=[("mm", mmc)], inc="dve")
            bank_free[bk] = ("dve", c)
    assert S.now("mm") == mm_bias_end, (S.now("mm"), mm_bias_end)




def _stream_plan(ctx, phases, mm_free, stg, par0=0):
    S = ctx.S
    gp0 = S.now("gp")
    cast_done = {}
    g = par0
    for key, chunks in phases:
        freq = mm_free.get(key)
        first = True
        for srcap, dstap in chunks:
            par = g % 2
            psem = "dma%d" % par
            nc_ = srcap.shape[-1]
            w = ([("gp", gp0 + (g - par0) - 1)] if (g - par0) >= 2
                 else [("gp", gp0)])
            dcnt = S.add("sync", lambda e2, srcap=srcap, par=par, nc_=nc_:
                         e2.dma_start(stg[par][:, 0:nc_], srcap),
                         waits=w, inc=psem)
            w = [(psem, dcnt)]
            if first and freq is not None:
                w.append(("mm", freq))
            first = False
            c = S.add("gpsimd", lambda e2, dstap=dstap, par=par, nc_=nc_:
                      e2.tensor_copy(dstap, stg[par][:, 0:nc_]),
                      waits=w, inc="gp")
            g += 1
        cast_done[key] = c
    return cast_done


def _chunks_cols(dram2d, dest, ncols, rows=None):
    rows = dram2d.shape[0] if rows is None else rows
    out = []
    for c in range(rows // 128):
        out.append((dram2d[c * 128:(c + 1) * 128, :],
                    dest[:, c * ncols:(c + 1) * ncols]))
    return out


def _emit_routing_transpose(ctx, S, wtm, wT8, wtm_done):
    tr_dve = []
    for r in range(2):
        for k in range(8):
            tt = r * 8 + k
            w = [("dve", wtm_done)] if r == 0 else [("dve", tr_dve[0])]
            S.add("tensor", lambda e, tt=tt, k=k: e.transpose(
                ctx.pbB[7][0:8, k * 128:(k + 1) * 128],
                wtm[:, tt * 8:(tt + 1) * 8], cstb(ctx, "ident")),
                waits=(w if k == 0 else []))
        S.add("tensor", lambda e: e.matmul(
            ctx.pb[6][0:1, 0:8], cstb(ctx, "ones_col"), wtm[:, 0:8],
            start=True, stop=True), inc="mm")
        tr_dve.append(S.add("vector", lambda e, r=r: e.tensor_copy(
            wT8[0:8, r * 1024:(r + 1) * 1024], ctx.pbB[7][0:8, 0:1024]),
            waits=[("mm", S.now("mm"))], inc="dve"))
    return tr_dve[-1]


def emit_f1(ctx, cv, L, xn_ready, coef, dbg=None):
    S = ctx.S
    E = 8
    stg = [cv.f32(1024) for _ in range(2)]
    wa = cv.bf16(8 * 1024)
    wb = cv.bf16(8 * 1024)
    wc = cv.bf16(8 * 1024)
    h = cv.bf16(8 * 512)
    ssc = [cv.f32(512) for _ in range(2)]
    wtm = cv.bf16(NTT * 8)
    wT8 = cv.bf16(T, parts=8)
    wbc = cv.bf16(NTC * 512)
    t1 = cv.f32(512)
    gsc = cv.f32(NTT * 8)
    msc = cv.f32(NTT * 8)
    srw = cv.f32(4 * NTT)
    wgb = cv.bf16(64)
    smf = cv.f32(64)

    S.add("sync", lambda e: e.dma_start(smf[:, 0:64], L["p_Wg"][:, :]),
          inc="dma")
    dsm = S.now("dma")
    S.add("sync", lambda e: None, waits=[("dma", dsm)])
    smalls_dve = S.add("vector", lambda e: e.tensor_copy(
        wgb, smf[:, 0:64]), waits=[("dma", dsm)], inc="dve")

    units = ["sh"] + list(range(E))
    if dbg == "sh":
        units = ["sh"]
    elif dbg == "e0":
        units = [0]
    if dbg == "wtm":
        units = []
    mm0 = S.now("mm")
    mm_router_end = mm0 + NTT
    mm_c = mm_router_end + 2
    mm_wbc_end = {}
    mm_phase_end = {}
    for u in units:
        if u != "sh":
            mm_c += NTC
            mm_wbc_end[u] = mm_c
        for hh in range(2):
            mm_c += NTC * 16
            mm_phase_end[(u, hh)] = mm_c
    mm_end = mm_c

    def prev_phase(u, hh):
        i = units.index(u)
        if hh == 1:
            return (u, 0)
        if i == 0:
            return None
        return (units[i - 1], 1)

    FF = 2048
    phases = []
    mm_free = {}
    for u in units:
        W1s = L["Ws1"] if u == "sh" else None
        for hh in range(2):
            key = (u, hh)
            ch = []
            for name, dest in (("1", wa), ("3", wb)):
                Wd = L["Ws" + name] if u == "sh" else L["We" + name]
                for dc in range(DC):
                    srcap = (Wd[dc * 128:(dc + 1) * 128,
                                hh * 1024:(hh + 1) * 1024] if u == "sh" else
                             Wd[u, dc * 128:(dc + 1) * 128,
                                hh * 1024:(hh + 1) * 1024])
                    ch.append((srcap, dest[:, dc * 1024:(dc + 1) * 1024]))
            W2d = L["Ws2"] if u == "sh" else L["We2"]
            for hc in range(8):
                r0 = hh * 1024 + hc * 128
                srcap = (W2d[r0:r0 + 128, :] if u == "sh" else
                         W2d[u, r0:r0 + 128, :])
                ch.append((srcap, wc[:, hc * 1024:(hc + 1) * 1024]))
            pp = prev_phase(u, hh)
            mm_free[key] = mm_phase_end[pp] if pp is not None else None
            phases.append((key, ch))
    cast_done = _stream_plan(ctx, phases, mm_free, stg)

    for tt in range(NTT):
        psl = ctx.pb[tt // 4][:, (tt % 4) * 8:(tt % 4) * 8 + 8]
        for dc in range(DC):
            mmc = S.add("tensor", lambda e, tt=tt, dc=dc, psl=psl: e.matmul(
                psl, ctx.AXN[:, dc * T + tt * 128: dc * T + (tt + 1) * 128],
                wgb[:, dc * 8:(dc + 1) * 8], start=(dc == 0),
                stop=(dc == DC - 1)),
                waits=[("dve", max(xn_ready, smalls_dve))],
                inc=("mm" if dc == DC - 1 else None))
    assert S.now("mm") == mm_router_end
    a_r = []
    for tt in range(NTT):
        psl = ctx.pb[tt // 4][:, (tt % 4) * 8:(tt % 4) * 8 + 8]
        a_r.append(S.add("scalar", lambda e, tt=tt, psl=psl: e.activation(
            gsc[:, tt * 8:(tt + 1) * 8], psl, AF.Sigmoid),
            waits=[("mm", mm0 + (tt // 4 + 1) * 4)], inc="act"))
    for tt in range(NTT):
        S.add("vector", lambda e, tt=tt: e.max(
            msc[:, tt * 8:(tt + 1) * 8], gsc[:, tt * 8:(tt + 1) * 8]),
            waits=[("act", a_r[tt])])
    wsum = srw[:, 0:NTT]
    winv = srw[:, NTT:2 * NTT]
    v1n = srw[:, 2 * NTT:3 * NTT]
    v2n = srw[:, 3 * NTT:4 * NTT]
    for tt in range(NTT):
        S.add("vector", lambda e, tt=tt: e.tensor_tensor(
            wsum[:, tt:tt + 1], msc[:, tt * 8:tt * 8 + 1],
            msc[:, tt * 8 + 1:tt * 8 + 2], ALU.add))
    for tt in range(NTT):
        S.add("vector", lambda e, tt=tt: e.reciprocal(
            winv[:, tt:tt + 1], wsum[:, tt:tt + 1]))
    for tt in range(NTT):
        S.add("vector", lambda e, tt=tt: e.tensor_tensor(
            v1n[:, tt:tt + 1], msc[:, tt * 8:tt * 8 + 1],
            winv[:, tt:tt + 1], ALU.mult))
    for tt in range(NTT):
        S.add("vector", lambda e, tt=tt: e.tensor_tensor(
            v2n[:, tt:tt + 1], msc[:, tt * 8 + 1:tt * 8 + 2],
            winv[:, tt:tt + 1], ALU.mult), inc="dve")
    S.selfflush("vector")
    ta = cv.f32(NTT * 8)
    tb = cv.f32(NTT * 8)
    for tt in range(NTT):
        S.add("vector", lambda e, tt=tt: e.tensor_scalar(
            ta[:, tt * 8:(tt + 1) * 8], gsc[:, tt * 8:(tt + 1) * 8],
            msc[:, tt * 8:tt * 8 + 1],
            v1n[:, tt:tt + 1], ALU.is_equal, ALU.mult))
    for tt in range(NTT):
        S.add("vector", lambda e, tt=tt: e.tensor_scalar(
            tb[:, tt * 8:(tt + 1) * 8], gsc[:, tt * 8:(tt + 1) * 8],
            msc[:, tt * 8 + 1:tt * 8 + 2], v2n[:, tt:tt + 1],
            ALU.is_equal, ALU.mult), inc="dve")
    S.selfflush("vector")
    for tt in range(NTT):
        S.add("vector", lambda e, tt=tt: e.tensor_tensor(
            wtm[:, tt * 8:(tt + 1) * 8], ta[:, tt * 8:(tt + 1) * 8],
            tb[:, tt * 8:(tt + 1) * 8], ALU.add))
    wtm_done = S.add("vector", lambda e: e.tensor_copy(
        gsc[:, 0:8], wtm[:, 0:8]), inc="dve")
    if dbg == "wtm":
        S.add("scalar", lambda e: e.activation(
            ctx.AX[:, 0:NTT * 8], wtm, AF.Copy),
            waits=[("dve", wtm_done)], inc="act")
        S.add("scalar", lambda e: e.activation(
            ctx.AX[:, NTT * 8:2 * NTT * 8], msc, AF.Copy), inc="act")
        S.add("scalar", lambda e: e.activation(
            ctx.AX[:, 2 * NTT * 8:2 * NTT * 8 + NTT], wsum[:, 0:NTT],
            AF.Copy), inc="act")
        return
    wT8_done = _emit_routing_transpose(ctx, S, wtm, wT8, wtm_done)
    assert S.now("mm") == mm_router_end + 2

    bank_free = {i: ("dve", wtm_done) for i in range(4)}
    wbc_free = {4: ("dve", 0), 5: ("dve", 0)}
    ssc_free = [("dve", 0), ("dve", 0)]
    nss = 0

    for u in units:
        routed = u != "sh"
        if routed:
            for tc in range(NTC):
                bk = 4 + tc % 2
                mmc = S.add("tensor", lambda e2, u=u, tc=tc, bk=bk: e2.matmul(
                    ctx.pb[bk][:, :],
                    cstb(ctx, "sel8", 8)[:, u * 128:(u + 1) * 128],
                    wT8[0:8, tc * 512:(tc + 1) * 512], start=True, stop=True),
                    waits=[("dve", wT8_done), wbc_free[bk]], inc="mm")
                c = S.add("vector", lambda e2, tc=tc, bk=bk: e2.tensor_scalar(
                    wbc[:, tc * 512:(tc + 1) * 512], ctx.pb[bk][:, :],
                    coef, None, ALU.mult), waits=[("mm", mmc)], inc="dve")
                wbc_free[bk] = ("dve", c)
            assert S.now("mm") == mm_wbc_end[u]

        for hh in range(2):
            cd = cast_done[(u, hh)]
            for tc in range(NTC):
                for hc in range(8):
                    bka = (hc % 2) * 2
                    bkb = bka + 1
                    w = [("gp", cd), bank_free[bka], bank_free[bkb]]
                    for dc in range(DC):
                        S.add("tensor",
                              lambda e2, hc=hc, tc=tc, dc=dc, bka=bka:
                              e2.matmul(
                                  ctx.pb[bka][:, :],
                                  wa[:, dc * 1024 + hc * 128:
                                     dc * 1024 + (hc + 1) * 128],
                                  ctx.AXN[:, dc * T + tc * 512:
                                          dc * T + (tc + 1) * 512],
                                  start=(dc == 0), stop=(dc == DC - 1)),
                              waits=(w if dc == 0 else []))
                    for dc in range(DC):
                        mmc = S.add(
                            "tensor",
                            lambda e2, hc=hc, tc=tc, dc=dc, bkb=bkb:
                            e2.matmul(
                                ctx.pb[bkb][:, :],
                                wb[:, dc * 1024 + hc * 128:
                                   dc * 1024 + (hc + 1) * 128],
                                ctx.AXN[:, dc * T + tc * 512:
                                        dc * T + (tc + 1) * 512],
                                start=(dc == 0), stop=(dc == DC - 1)),
                            inc=("mm" if dc == DC - 1 else None))
                    sp = nss % 2
                    ac = S.add("scalar",
                               lambda e2, bka=bka, sp=sp: e2.activation(
                                   ssc[sp], ctx.pb[bka][:, :], AF.Silu),
                               waits=[("mm", mmc), ssc_free[sp]], inc="act")
                    c = S.add("vector",
                              lambda e2, hc=hc, bkb=bkb, sp=sp:
                              e2.tensor_tensor(
                                  h[:, hc * 512:(hc + 1) * 512], ssc[sp],
                                  ctx.pb[bkb][:, :], ALU.mult),
                              waits=[("act", ac)], inc="dve")
                    ssc_free[sp] = ("dve", c)
                    bank_free[bka] = ("dve", c)
                    bank_free[bkb] = ("dve", c)
                    nss += 1
                h_ready = S.now("dve")
                for dcy in range(DC):
                    bk = dcy % 4
                    w = [("dve", h_ready), bank_free[bk]]
                    for hc in range(8):
                        mmc = S.add(
                            "tensor",
                            lambda e2, dcy=dcy, hc=hc, bk=bk: e2.matmul(
                                ctx.pb[bk][:, :],
                                wc[:, hc * 1024 + dcy * 128:
                                   hc * 1024 + (dcy + 1) * 128],
                                h[:, hc * 512:(hc + 1) * 512],
                                start=(hc == 0), stop=(hc == 7)),
                            waits=(w if hc == 0 else []),
                            inc=("mm" if hc == 7 else None))
                    if routed:
                        c = S.add("vector",
                                  lambda e2, bk=bk, tc=tc: e2.tensor_tensor(
                                      t1, ctx.pb[bk][:, :],
                                      wbc[:, tc * 512:(tc + 1) * 512],
                                      ALU.mult),
                                  waits=[("mm", mmc)], inc="dve")
                        S.add("vector", lambda e2, dcy=dcy, tc=tc:
                              e2.tensor_tensor(
                                  ctx.xc(dcy, tc), ctx.xc(dcy, tc), t1,
                                  ALU.add))
                    else:
                        c = S.add("vector",
                                  lambda e2, dcy=dcy, tc=tc, bk=bk:
                                  e2.scalar_tensor_tensor(
                                      ctx.xc(dcy, tc), ctx.pb[bk][:, :],
                                      coef, ctx.xc(dcy, tc),
                                      ALU.mult, ALU.add),
                                  waits=[("mm", mmc)], inc="dve")
                    bank_free[bk] = ("dve", c)
            assert S.now("mm") == mm_phase_end[(u, hh)], \
                (u, hh, S.now("mm"), mm_phase_end[(u, hh)])
    assert S.now("mm") == mm_end


def emit_f3(ctx, cv, L, xn_ready, coef, ln_scr):
    S = ctx.S
    E, H = 8, 4096
    stg = [cv.f32(1024) for _ in range(2)]
    w1q = cv.bf16(8 * 1024)
    w2q = cv.bf16(8 * 1024)
    h = cv.bf16(8 * 512)
    wtm = cv.bf16(NTT * 8)
    wT8 = cv.bf16(T, parts=8)
    wbc = cv.bf16(NTC * 512)
    wrb = cv.bf16(64)
    t1 = cv.f32(512)
    lsc = cv.f32(NTT * 8)
    esc = cv.f32(NTT * 8)
    msc = cv.f32(NTT * 8)
    srw = cv.f32(4 * NTT)
    smf = cv.f32(64)
    gbcol = cv.f32(2 * DC)

    res_done = 0
    for dc in range(DC):
        for tc in range(NTC):
            res_done = S.add("vector", lambda e, dc=dc, tc=tc:
                             e.scalar_tensor_tensor(
                                 ctx.xc(dc, tc), ctx.xnc(dc, tc), coef,
                                 ctx.xc(dc, tc), ALU.mult, ALU.add),
                             waits=[("dve", xn_ready)], inc="dve")

    S.add("sync", lambda e: e.dma_start(smf[:, 0:64], L["p_router"][:, :]),
          inc="dma")
    S.add("sync", lambda e: e.dma_start(gbcol, L["p_gb"][:, :]), inc="dma")
    dsm = S.now("dma")
    S.add("sync", lambda e: None, waits=[("dma", dsm)])
    smalls_dve = S.add("vector", lambda e: e.tensor_copy(
        wrb, smf[:, 0:64]), waits=[("dma", dsm)], inc="dve")

    xn2_ready = emit_spine_ln(ctx, cv, src=ctx.AXN, dst=ctx.AXN,
                              gcol=gbcol[:, 0:DC], bcol=gbcol[:, DC:2 * DC],
                              scr=ln_scr)

    mm0 = S.now("mm")
    mm_router_end = mm0 + NTT
    mm_c = mm_router_end + 2
    mm_wbc_end = {}
    mm_phase_end = {}
    for e in range(E):
        mm_c += NTC
        mm_wbc_end[e] = mm_c
        for qq in range(4):
            mm_c += NTC * 16
            mm_phase_end[(e, qq)] = mm_c
    mm_end = mm_c

    def prev_phase(e, qq):
        if (e, qq) == (0, 0):
            return None
        return (e, qq - 1) if qq else (e - 1, 3)

    phases = []
    mm_free = {}
    for e in range(E):
        for qq in range(4):
            key = (e, qq)
            ch = []
            for dc in range(DC):
                ch.append((L["W1"][e, dc * 128:(dc + 1) * 128,
                                   qq * 1024:(qq + 1) * 1024],
                           w1q[:, dc * 1024:(dc + 1) * 1024]))
            for hc in range(8):
                r0 = qq * 1024 + hc * 128
                ch.append((L["W2"][e, r0:r0 + 128, :],
                           w2q[:, hc * 1024:(hc + 1) * 1024]))
            pp = prev_phase(e, qq)
            mm_free[key] = mm_phase_end[pp] if pp is not None else None
            phases.append((key, ch))
    cast_done = _stream_plan(ctx, phases, mm_free, stg)

    for tt in range(NTT):
        psl = ctx.pb[tt // 4][:, (tt % 4) * 8:(tt % 4) * 8 + 8]
        for dc in range(DC):
            S.add("tensor", lambda e, tt=tt, dc=dc, psl=psl: e.matmul(
                psl, ctx.AXN[:, dc * T + tt * 128: dc * T + (tt + 1) * 128],
                wrb[:, dc * 8:(dc + 1) * 8], start=(dc == 0),
                stop=(dc == DC - 1)),
                waits=[("dve", max(xn2_ready, smalls_dve))],
                inc=("mm" if dc == DC - 1 else None))
    assert S.now("mm") == mm_router_end
    rm = srw[:, 0:NTT]
    sume = srw[:, NTT:2 * NTT]
    pst = srw[:, 2 * NTT:3 * NTT]
    v_r, a_r = [], []
    for tt in range(NTT):
        psl = ctx.pb[tt // 4][:, (tt % 4) * 8:(tt % 4) * 8 + 8]
        S.add("vector", lambda e, tt=tt, psl=psl: e.tensor_reduce(
            rm[:, tt:tt + 1], psl, AX_.X, ALU.max),
            waits=[("mm", mm0 + (tt // 4 + 1) * 4)])
    for tt in range(NTT):
        psl = ctx.pb[tt // 4][:, (tt % 4) * 8:(tt % 4) * 8 + 8]
        v_r.append(S.add("vector", lambda e, tt=tt, psl=psl: e.tensor_scalar(
            lsc[:, tt * 8:(tt + 1) * 8], psl, rm[:, tt:tt + 1], None,
            ALU.subtract), inc="dve"))
    for tt in range(NTT):
        a_r.append(S.add("scalar", lambda e, tt=tt: e.activation(
            esc[:, tt * 8:(tt + 1) * 8], lsc[:, tt * 8:(tt + 1) * 8],
            AF.Exp, scale=1.0 / 1.5), waits=[("dve", v_r[tt])], inc="act"))
    for tt in range(NTT):
        S.add("vector", lambda e, tt=tt: e.tensor_reduce(
            sume[:, tt:tt + 1], esc[:, tt * 8:(tt + 1) * 8], AX_.X, ALU.add),
            waits=[("act", a_r[tt])])
    for tt in range(NTT):
        S.add("vector", lambda e, tt=tt: e.reciprocal(
            pst[:, tt:tt + 1], sume[:, tt:tt + 1]))
    for tt in range(NTT):
        S.add("vector", lambda e, tt=tt: e.max(
            msc[:, tt * 8:(tt + 1) * 8], esc[:, tt * 8:(tt + 1) * 8]))
    v1n = srw[:, 3 * NTT:4 * NTT]
    v2n = lsc[:, 0:NTT]
    for tt in range(NTT):
        S.add("vector", lambda e, tt=tt: e.tensor_tensor(
            v1n[:, tt:tt + 1], msc[:, tt * 8:tt * 8 + 1],
            pst[:, tt:tt + 1], ALU.mult))
    for tt in range(NTT):
        S.add("vector", lambda e, tt=tt: e.tensor_tensor(
            v2n[:, tt:tt + 1], msc[:, tt * 8 + 1:tt * 8 + 2],
            pst[:, tt:tt + 1], ALU.mult), inc="dve")
    S.selfflush("vector")
    ta = cv.f32(NTT * 8)
    tb = cv.f32(NTT * 8)
    for tt in range(NTT):
        S.add("vector", lambda e, tt=tt: e.tensor_scalar(
            ta[:, tt * 8:(tt + 1) * 8], esc[:, tt * 8:(tt + 1) * 8],
            msc[:, tt * 8:tt * 8 + 1],
            v1n[:, tt:tt + 1], ALU.is_equal, ALU.mult))
    for tt in range(NTT):
        S.add("vector", lambda e, tt=tt: e.tensor_scalar(
            tb[:, tt * 8:(tt + 1) * 8], esc[:, tt * 8:(tt + 1) * 8],
            msc[:, tt * 8 + 1:tt * 8 + 2], v2n[:, tt:tt + 1],
            ALU.is_equal, ALU.mult), inc="dve")
    S.selfflush("vector")
    for tt in range(NTT):
        S.add("vector", lambda e, tt=tt: e.tensor_tensor(
            wtm[:, tt * 8:(tt + 1) * 8], ta[:, tt * 8:(tt + 1) * 8],
            tb[:, tt * 8:(tt + 1) * 8], ALU.add))
    wtm_done = S.add("vector", lambda e: e.tensor_copy(
        esc[:, 0:8], wtm[:, 0:8]), inc="dve")
    wT8_done = _emit_routing_transpose(ctx, S, wtm, wT8, wtm_done)
    assert S.now("mm") == mm_router_end + 2

    bank_free = {i: ("dve", wtm_done) for i in range(4)}
    wbc_free = {4: ("dve", 0), 5: ("dve", 0)}
    for e in range(E):
        for tc in range(NTC):
            bk = 4 + tc % 2
            mmc = S.add("tensor", lambda e2, e=e, tc=tc, bk=bk: e2.matmul(
                ctx.pb[bk][:, :],
                cstb(ctx, "sel8", 8)[:, e * 128:(e + 1) * 128],
                wT8[0:8, tc * 512:(tc + 1) * 512], start=True, stop=True),
                waits=[("dve", wT8_done), wbc_free[bk]], inc="mm")
            c = S.add("vector", lambda e2, tc=tc, bk=bk: e2.tensor_scalar(
                wbc[:, tc * 512:(tc + 1) * 512], ctx.pb[bk][:, :],
                coef, None, ALU.mult), waits=[("mm", mmc)], inc="dve")
            wbc_free[bk] = ("dve", c)
        assert S.now("mm") == mm_wbc_end[e]

        for qq in range(4):
            cd = cast_done[(e, qq)]
            for tc in range(NTC):
                for hc in range(8):
                    bk = hc % 4
                    w = [("gp", cd), bank_free[bk]]
                    for dc in range(DC):
                        mmc = S.add(
                            "tensor",
                            lambda e2, hc=hc, tc=tc, dc=dc, bk=bk: e2.matmul(
                                ctx.pb[bk][:, :],
                                w1q[:, dc * 1024 + hc * 128:
                                    dc * 1024 + (hc + 1) * 128],
                                ctx.AXN[:, dc * T + tc * 512:
                                        dc * T + (tc + 1) * 512],
                                start=(dc == 0), stop=(dc == DC - 1)),
                            waits=(w if dc == 0 else []),
                            inc=("mm" if dc == DC - 1 else None))
                    c = S.add("scalar", lambda e2, hc=hc, bk=bk:
                              e2.activation(h[:, hc * 512:(hc + 1) * 512],
                                            ctx.pb[bk][:, :], AF.Silu),
                              waits=[("mm", mmc)], inc="act")
                    bank_free[bk] = ("act", c)
                h_ready = ("act", S.now("act"))
                for dcy in range(DC):
                    bk = dcy % 4
                    w = [h_ready, bank_free[bk]]
                    for hc in range(8):
                        mmc = S.add(
                            "tensor",
                            lambda e2, dcy=dcy, hc=hc, bk=bk: e2.matmul(
                                ctx.pb[bk][:, :],
                                w2q[:, hc * 1024 + dcy * 128:
                                    hc * 1024 + (dcy + 1) * 128],
                                h[:, hc * 512:(hc + 1) * 512],
                                start=(hc == 0), stop=(hc == 7)),
                            waits=(w if hc == 0 else []),
                            inc=("mm" if hc == 7 else None))
                    c = S.add("vector", lambda e2, bk=bk, tc=tc:
                              e2.tensor_tensor(
                                  t1, ctx.pb[bk][:, :],
                                  wbc[:, tc * 512:(tc + 1) * 512], ALU.mult),
                              waits=[("mm", mmc)], inc="dve")
                    S.add("vector", lambda e2, dcy=dcy, tc=tc:
                          e2.tensor_tensor(
                              ctx.xc(dcy, tc), ctx.xc(dcy, tc), t1,
                              ALU.add))
                    bank_free[bk] = ("dve", c)
            assert S.now("mm") == mm_phase_end[(e, qq)], \
                (e, qq, S.now("mm"), mm_phase_end[(e, qq)])
    assert S.now("mm") == mm_end




def emit_a1(ctx, cv, L):
    S = ctx.S
    NH = 16
    lnbase = cv.off
    scr = make_ln_scratch(cv)
    xn_ready = emit_spine_ln(ctx, cv, scr=scr)

    ctxu = cv.bf16(DC * T)
    hmark = cv.off
    uh = cv.bf16(T, parts=64)
    qh = cv.bf16(T, parts=64)
    kh = cv.bf16(T, parts=64)
    vh = cv.bf16(NTT * 64)
    P = cv.bf16(T)
    PT = cv.bf16(T)
    whb = cv.bf16(8 * 256)
    stg = [cv.f32(512) for _ in range(2)]
    t1 = cv.f32(128)
    t2 = cv.f32(128)
    cols = cv.f32(3 * DC)
    b1h = cv.f32(4 * NH, parts=64)
    b1vr_f = cv.f32(1024, parts=1)
    b1vr = cv.bf16(1024, parts=1)

    S.add("sync", lambda e: e.dma_start(cols, L["p_cols"][:, :]), inc="dma")
    S.add("sync", lambda e: e.dma_start(b1h, L["p_b1h"][:, :]), inc="dma")
    S.add("sync", lambda e: e.dma_start(b1vr_f, L["p_b1vrow"][:, :]),
          inc="dma")
    dsm = S.now("dma")
    S.add("sync", lambda e: None, waits=[("dma", dsm)])
    bvc = S.add("vector", lambda e: e.tensor_copy(b1vr, b1vr_f),
                waits=[("dma", dsm)], inc="dve")

    mm0 = S.now("mm")
    mm_head_end = {}
    mm_c = mm0
    for h in range(NH):
        mm_c += 12 + NTT
        mm_c += NTT * (NTC + 2 + 1)
        mm_head_end[h] = mm_c
    phases = []
    mm_free = {}
    for h in range(NH):
        ch = []
        for dc in range(DC):
            for j, cbase in ((0, h * 64), (1, 1024 + h * 64),
                             (2, 2048 + h * 64), (3, 3072 + h * 64)):
                ch.append((L["W1"][dc * 128:(dc + 1) * 128,
                                   cbase:cbase + 64],
                           whb[:, dc * 256 + j * 64:dc * 256 + (j + 1) * 64]))
        mm_free[h] = mm_head_end[h - 1] if h else None
        phases.append((h, ch))
    cast_done = _stream_plan(ctx, phases, mm_free, stg)

    bank_free = {i: ("dve", 0) for i in range(8)}
    t2_free = ("act", 0)
    for h in range(NH):
        cd = cast_done[h]
        pbh = (h % 2) * 64
        chh = h // 2
        dve_head = S.now("dve")
        for j, tgt in ((0, uh), (2, qh), (3, kh)):
            for tc in range(NTC):
                bk = tc % 4
                w = [("gp", cd), bank_free[bk], ("dve", xn_ready)]
                for dc in range(DC):
                    mmc = S.add(
                        "tensor", lambda e, j=j, tc=tc, dc=dc, bk=bk:
                        e.matmul(
                            ctx.pb[bk][0:64, :],
                            whb[:, dc * 256 + j * 64:dc * 256 + (j + 1) * 64],
                            ctx.AXN[:, dc * T + tc * 512:
                                    dc * T + (tc + 1) * 512],
                            start=(dc == 0), stop=(dc == DC - 1)),
                        waits=(w if dc == 0 else []),
                        inc=("mm" if dc == DC - 1 else None))
                c = S.add("scalar",
                          lambda e, tgt=tgt, tc=tc, bk=bk, h=h, j=j:
                          e.activation(
                              tgt[:, tc * 512:(tc + 1) * 512],
                              ctx.pb[bk][0:64, :], AF.Silu,
                              bias=b1h[:, h * 4 + j:h * 4 + j + 1]),
                          waits=[("mm", mmc), ("dve", dve_head)], inc="act")
                bank_free[bk] = ("act", c)
        for tt in range(NTT):
            bk = tt % 4
            w = [("gp", cd), bank_free[bk], ("dve", xn_ready)]
            for dc in range(DC):
                S.add("tensor", lambda e, tt=tt, dc=dc, bk=bk: e.matmul(
                    ctx.pb[bk][:, 0:64],
                    ctx.AXN[:, dc * T + tt * 128: dc * T + (tt + 1) * 128],
                    whb[:, dc * 256 + 64:dc * 256 + 128],
                    start=(dc == 0), stop=False),
                    waits=(w if dc == 0 else []))
            mmc = S.add("tensor", lambda e, h=h, bk=bk: e.matmul(
                ctx.pb[bk][:, 0:64], cstb(ctx, "ones_row", 1),
                b1vr[0:1, h * 64:(h + 1) * 64],
                start=False, stop=True), waits=[("dve", bvc)], inc="mm")
            c = S.add("scalar", lambda e, tt=tt, bk=bk: e.activation(
                vh[:, tt * 64:(tt + 1) * 64], ctx.pb[bk][:, 0:64],
                AF.Silu), waits=[("mm", mmc)], inc="act")
            bank_free[bk] = ("act", c)
        for qt in range(NTT):
            sc_mm = 0
            for kc4 in range(NTC):
                bk = kc4
                sc_mm = S.add("tensor",
                              lambda e, qt=qt, kc4=kc4, bk=bk: e.matmul(
                                  ctx.pb[bk][:, :],
                                  qh[:, qt * 128:(qt + 1) * 128],
                                  kh[:, kc4 * 512:(kc4 + 1) * 512],
                                  start=True, stop=True),
                              waits=[bank_free[bk]], inc="mm")
            pa = 0
            for kc4 in range(NTC):
                pa = S.add("scalar", lambda e, kc4=kc4: e.activation(
                    P[:, kc4 * 512:(kc4 + 1) * 512], ctx.pb[kc4][:, :],
                    AF.Silu), waits=[("mm", sc_mm)], inc="act")
                bank_free[kc4] = ("act", pa)
            tcopy = []
            for r in range(2):
                bkt = 4 + r
                for kk in range(8):
                    kc = r * 8 + kk
                    w = ([("act", pa), bank_free[bkt]] if kk == 0 else [])
                    S.add("tensor",
                          lambda e, kc=kc, kk=kk, bkt=bkt: e.transpose(
                              ctx.pbB[bkt][:, kk * 128:(kk + 1) * 128],
                              P[:, kc * 128:(kc + 1) * 128],
                              cstb(ctx, "ident")), waits=w)
                mmt = S.add("tensor", lambda e: e.matmul(
                    ctx.pb[6][0:1, 0:8], cstb(ctx, "ones_col"), P[:, 0:8],
                    start=True, stop=True), inc="mm")
                c = S.add("vector", lambda e, r=r, bkt=bkt: e.tensor_copy(
                    PT[:, r * 1024:(r + 1) * 1024],
                    ctx.pbB[bkt][0:128, 0:1024]),
                    waits=[("mm", mmt)], inc="dve")
                bank_free[bkt] = ("dve", c)
                tcopy.append(c)
            avw = [("dve", tcopy[-1]), bank_free[7]]
            mma = 0
            for kc in range(NTT):
                mma = S.add("tensor", lambda e, kc=kc: e.matmul(
                    ctx.pb[7][0:64, 0:128],
                    vh[:, kc * 64:(kc + 1) * 64],
                    PT[:, kc * 128:(kc + 1) * 128],
                    start=(kc == 0), stop=(kc == NTT - 1)),
                    waits=(avw if kc == 0 else []),
                    inc=("mm" if kc == NTT - 1 else None))
            c = S.add("vector", lambda e, qt=qt: e.tensor_scalar(
                t1[0:64, :], ctx.pb[7][0:64, 0:128], 1.0 / T, None,
                ALU.mult), waits=[("mm", mma), t2_free], inc="dve")
            bank_free[7] = ("dve", c)
            c2 = S.add("vector", lambda e, qt=qt: e.tensor_tensor(
                t2[0:64, :], t1[0:64, :],
                uh[:, qt * 128:(qt + 1) * 128], ALU.mult), inc="dve")
            ca = S.add("scalar", lambda e, pbh=pbh, chh=chh, qt=qt:
                       e.activation(
                           ctxu[pbh:pbh + 64, chh * T + qt * 128:
                                chh * T + (qt + 1) * 128],
                           t2[0:64, :], AF.Copy),
                       waits=[("dve", c2)], inc="act")
            t2_free = ("act", ca)
        assert S.now("mm") == mm_head_end[h], (h, S.now("mm"),
                                               mm_head_end[h])

    hv2 = cv.fork(hmark)
    scr2 = make_ln_scratch(hv2)
    ln2 = emit_spine_ln(ctx, cv, src=ctxu, dst=ctxu,
                        gcol=cols[:, DC:2 * DC], bcol=cols[:, 2 * DC:3 * DC],
                        scr=scr2)

    S.add("sync", lambda e: None,
          waits=[("mm", S.now("mm")), ("act", S.now("act")),
                 ("dve", S.now("dve"))])
    hv3 = cv.fork(lnbase, lnbase + 22528)
    stg2 = [hv3.f32(512) for _ in range(2)]
    w2b = hv3.bf16(8 * 1024)
    w2ch = []
    for dc in range(DC):
        for hf in range(2):
            w2ch.append((L["W2"][dc * 128:(dc + 1) * 128,
                                 hf * 512:(hf + 1) * 512],
                         w2b[:, dc * 1024 + hf * 512:
                             dc * 1024 + (hf + 1) * 512]))
    phases = [("w2", w2ch)]
    cast_done = _stream_plan(ctx, phases, {"w2": S.now("mm")}, stg2)
    cd = cast_done["w2"]
    t1b = hv3.f32(512)
    for dco in range(DC):
        for tc in range(NTC):
            bk = (dco * NTC + tc) % 4
            w = [("gp", cd), bank_free[bk], ("dve", ln2)]
            for dcw in range(DC):
                mmc = S.add("tensor",
                            lambda e, dco=dco, tc=tc, dcw=dcw, bk=bk:
                            e.matmul(
                                ctx.pb[bk][:, :],
                                w2b[:, dcw * 1024 + dco * 128:
                                    dcw * 1024 + (dco + 1) * 128],
                                ctxu[:, dcw * T + tc * 512:
                                     dcw * T + (tc + 1) * 512],
                                start=(dcw == 0), stop=(dcw == DC - 1)),
                            waits=(w if dcw == 0 else []),
                            inc=("mm" if dcw == DC - 1 else None))
            c = S.add("vector", lambda e, dco=dco, bk=bk: e.tensor_scalar(
                t1b, ctx.pb[bk][:, :], cols[:, dco:dco + 1], None,
                ALU.add), waits=[("mm", mmc)], inc="dve")
            S.add("vector", lambda e, dco=dco, tc=tc: e.tensor_tensor(
                ctx.xc(dco, tc), ctx.xc(dco, tc), t1b, ALU.add))
            bank_free[bk] = ("dve", c)




def emit_f2(ctx, cv, L, xn_ready, coef):
    S = ctx.S
    stg = [cv.f32(1024) for _ in range(2)]
    wfc = cv.bf16(8 * 1024)
    wpj = cv.bf16(8 * 1024)
    h = cv.bf16(8 * 512)
    ysb = cv.f32(8 * 512)
    rl = [cv.f32(512) for _ in range(2)]
    smem = {"vector": [cv.f32(512) for _ in range(3)],
            "gpsimd": [cv.f32(512) for _ in range(3)]}

    mm0 = S.now("mm")
    mm_phase_end = {}
    mm_c = mm0
    for tp in range(4):
        for qq in range(4):
            mm_c += 16
            mm_phase_end[(tp, qq)] = mm_c

    def pass_phases(tp, prev):
        phases = []
        mm_free = {}
        for qq in range(4):
            key = (tp, qq)
            ch = []
            for dc in range(DC):
                for hf in range(2):
                    ch.append((L["WfcT"][dc * 128:(dc + 1) * 128,
                                         qq * 1024 + hf * 512:
                                         qq * 1024 + (hf + 1) * 512],
                               wfc[:, dc * 1024 + hf * 512:
                                   dc * 1024 + (hf + 1) * 512]))
            for hc in range(8):
                r0 = qq * 1024 + hc * 128
                ch.append((L["WprojT"][r0:r0 + 128, :],
                           wpj[:, hc * 1024:(hc + 1) * 1024]))
            mm_free[key] = mm_phase_end[prev] if prev is not None else None
            prev = key
            phases.append((key, ch))
        return phases, mm_free, prev

    bank_free = {i: ("dve", 0) for i in range(8)}
    rl_free = [("act", 0), ("act", 0)]
    ysb_rd = 0
    gp_sp = 0
    prev = None
    for tp in range(4):
        tc = tp
        phases, mm_free, prev = pass_phases(tp, prev)
        cast_done = _stream_plan(ctx, phases, mm_free, stg)
        ysb_done = 0
        for qq in range(4):
            cd = cast_done[(tp, qq)]
            for hc in range(8):
                bk = hc % 4
                w = [("gp", cd), bank_free[bk], ("dve", xn_ready)]
                for dc in range(DC):
                    mmc = S.add(
                        "tensor", lambda e, hc=hc, dc=dc, bk=bk, tc=tc:
                        e.matmul(
                            ctx.pb[bk][:, :],
                            wfc[:, dc * 1024 + hc * 128:
                                dc * 1024 + (hc + 1) * 128],
                            ctx.AXN[:, dc * T + tc * 512:
                                    dc * T + (tc + 1) * 512],
                            start=(dc == 0), stop=(dc == DC - 1)),
                        waits=(w if dc == 0 else []),
                        inc=("mm" if dc == DC - 1 else None))
                par = hc % 2
                c = S.add("vector", lambda e, bk=bk, par=par: e.tensor_scalar(
                    rl[par], ctx.pb[bk][:, :], 0.0, None, ALU.max),
                    waits=[("mm", mmc), rl_free[par]], inc="dve")
                bank_free[bk] = ("dve", c)
                ca = S.add("scalar", lambda e, hc=hc, par=par: e.activation(
                    h[:, hc * 512:(hc + 1) * 512], rl[par], AF.Square),
                    waits=[("dve", c)], inc="act")
                rl_free[par] = ("act", ca)
            h_ready = ("act", S.now("act"))
            for dco in range(DC):
                bk = dco % 4
                w = [h_ready, bank_free[bk]]
                for hc in range(8):
                    mmc = S.add(
                        "tensor", lambda e, dco=dco, hc=hc, bk=bk: e.matmul(
                            ctx.pb[bk][:, :],
                            wpj[:, hc * 1024 + dco * 128:
                                hc * 1024 + (dco + 1) * 128],
                            h[:, hc * 512:(hc + 1) * 512],
                            start=(hc == 0), stop=(hc == 7)),
                        waits=(w if hc == 0 else []),
                        inc=("mm" if hc == 7 else None))
                ysl = ysb[:, dco * 512:(dco + 1) * 512]
                if qq == 0:
                    c = S.add("vector", lambda e, bk=bk, ysl=ysl:
                              e.tensor_scalar(ysl, ctx.pb[bk][:, :], 1.0,
                                              None, ALU.mult),
                              waits=[("mm", mmc), ("dve", ysb_rd),
                                     ("gp", gp_sp)],
                              inc="dve")
                else:
                    c = S.add("vector", lambda e, bk=bk, ysl=ysl:
                              e.tensor_tensor(ysl, ysl, ctx.pb[bk][:, :],
                                              ALU.add),
                              waits=[("mm", mmc)], inc="dve")
                bank_free[bk] = ("dve", c)
            assert S.now("mm") == mm_phase_end[(tp, qq)]
            ysb_done = S.now("dve")
        for dco in range(DC):
            eng = "vector" if dco % 2 == 0 else "gpsimd"
            sem = "dve" if eng == "vector" else "gp"
            mem, acc, rr = smem[eng]
            ysl = ysb[:, dco * 512:(dco + 1) * 512]
            w0 = [("dve", ysb_done)]
            S.add(eng, lambda e, ysl=ysl, mem=mem: e.tensor_scalar(
                mem, ysl, 1.0, None, ALU.mult), waits=w0)
            S.add(eng, lambda e, ysl=ysl, acc=acc: e.tensor_scalar(
                acc, ysl, 1.0, None, ALU.is_gt))
            for st in range(4):
                S.add(eng, lambda e, mem=mem, rr=rr: e.tensor_scalar(
                    rr, mem, 1.0, None, ALU.is_gt))
                S.add(eng, lambda e, mem=mem: e.tensor_scalar(
                    mem, mem, 0.9, None, ALU.mult))
                S.add(eng, lambda e, mem=mem, ysl=ysl: e.tensor_tensor(
                    mem, mem, ysl, ALU.add))
                S.add(eng, lambda e, mem=mem, rr=rr: e.tensor_tensor(
                    mem, mem, rr, ALU.subtract))
                S.add(eng, lambda e, mem=mem, rr=rr: e.tensor_scalar(
                    rr, mem, 1.0, None, ALU.is_gt))
                S.add(eng, lambda e, acc=acc, rr=rr: e.tensor_tensor(
                    acc, acc, rr, ALU.add))
            S.add(eng, lambda e, acc=acc: e.tensor_scalar(
                acc, acc, coef / 5.0, None, ALU.mult))
            S.add(eng, lambda e, acc=acc, dco=dco, tc=tc: e.tensor_tensor(
                ctx.xc(dco, tc), ctx.xc(dco, tc), acc, ALU.add), inc=sem)
        ysb_rd = S.now("dve")
        gp_sp = S.now("gp")




def emit_a3(ctx, cv, L):
    S = ctx.S
    NH = 16
    lnbase = cv.off
    scr = make_ln_scratch(cv)
    xn_ready = emit_spine_ln(ctx, cv, scr=scr)

    ctx_a = cv.bf16(DC * T)
    lat = cv.bf16(2 * T)
    wdownb = cv.bf16(8 * 256)
    wupkb = cv.bf16(2 * 1024)
    wupvb = cv.bf16(2 * 1024)
    stg = [cv.f32(512) for _ in range(2)]
    wkvmark = cv.off
    wkvb = cv.bf16(8 * 1024)
    av1 = cv.fork(lnbase, lnbase + 22528)
    sc = av1.f32(T)
    P = av1.bf16(T)
    P2 = av1.bf16(T)
    dk = [av1.bf16(T)]
    kvst = av1.bf16(512)
    rows = av1.f32(2)
    rm = rows[:, 0:1]
    ngr = rows[:, 1:2]
    hv = cv.fork(wkvmark)
    qh = hv.bf16(T, parts=64)
    krh = hv.bf16(T, parts=64)
    vh = hv.bf16(NTT * 64)
    whb = hv.bf16(8 * 64)
    wub = hv.bf16(2 * 128)
    PT = hv.bf16(T)

    sm_phases = [("wd", [(L["Wdown"][dc * 128:(dc + 1) * 128, :],
                          wdownb[:, dc * 256:(dc + 1) * 256])
                         for dc in range(DC)]),
                 ("wuk", [(L["Wupk"][ch * 128:(ch + 1) * 128,
                                     hf * 512:(hf + 1) * 512],
                           wupkb[:, ch * 1024 + hf * 512:
                                 ch * 1024 + (hf + 1) * 512])
                          for ch in range(2) for hf in range(2)]),
                 ("wuv", [(L["Wupv"][ch * 128:(ch + 1) * 128,
                                     hf * 512:(hf + 1) * 512],
                           wupvb[:, ch * 1024 + hf * 512:
                                 ch * 1024 + (hf + 1) * 512])
                          for ch in range(2) for hf in range(2)])]
    cast_sm = _stream_plan(ctx, sm_phases, {}, stg)

    base_dma = {"k": None}
    for dc in range(DC):
        for hf in range(2):
            sl = slice(hf * 512, (hf + 1) * 512)
            d1 = S.add("sync", lambda e, dc=dc, sl=sl: e.dma_start(
                stg[0], L["Wk"][dc * 128:(dc + 1) * 128, sl]),
                waits=[("dve", S.now("dve"))], inc="dma0")
            d2 = S.add("sync", lambda e, dc=dc, sl=sl: e.dma_start(
                stg[1], L["Wv"][dc * 128:(dc + 1) * 128, sl]),
                inc="dma1")
            c = S.add("vector", lambda e, dc=dc, hf=hf:
                      e.tensor_tensor(
                          wkvb[:, dc * 1024 + hf * 512:
                               dc * 1024 + (hf + 1) * 512],
                          stg[0], stg[1], ALU.add),
                      waits=[("dma0", d1), ("dma1", d2)], inc="dve")
    wkv_done = S.now("dve")

    lat_done = 0
    bank_free = {i: ("dve", 0) for i in range(8)}
    for tc in range(NTC):
        for dc2 in range(DC):
            bk = dc2 % 4
            w = [("dve", max(wkv_done, xn_ready)), bank_free[bk]]
            for dc in range(DC):
                mmc = S.add("tensor",
                            lambda e, dc2=dc2, dc=dc, tc=tc, bk=bk:
                            e.matmul(
                                ctx.pb[bk][:, :],
                                wkvb[:, dc * 1024 + dc2 * 128:
                                     dc * 1024 + (dc2 + 1) * 128],
                                ctx.AXN[:, dc * T + tc * 512:
                                        dc * T + (tc + 1) * 512],
                                start=(dc == 0), stop=(dc == DC - 1)),
                            waits=(w if dc == 0 else []),
                            inc=("mm" if dc == DC - 1 else None))
            ka = S.add("scalar", lambda e, bk=bk: e.activation(
                kvst, ctx.pb[bk][:, :], AF.Copy), waits=[("mm", mmc)],
                inc="act")
            bank_free[bk] = ("act", ka)
            for lc in range(2):
                mmc = S.add("tensor",
                            lambda e, dc2=dc2, lc=lc, ka=ka: e.matmul(
                                ctx.pb[4 + lc][:, :],
                                wdownb[:, dc2 * 256 + lc * 128:
                                       dc2 * 256 + (lc + 1) * 128],
                                kvst[:, :], start=(dc2 == 0),
                                stop=(dc2 == DC - 1)),
                            waits=([("act", ka),
                                    ("gp", cast_sm["wd"])] if True else []),
                            inc=("mm" if dc2 == DC - 1 else None))
        for lc in range(2):
            lat_done = S.add("scalar", lambda e, lc=lc, tc=tc: e.activation(
                lat[:, lc * T + tc * 512:lc * T + (tc + 1) * 512],
                ctx.pb[4 + lc][:, :], AF.Copy), waits=[("mm", mmc)],
                inc="act")

    mm0 = S.now("mm")
    mm_head_end = {}
    mm_c = mm0
    for h in range(NH):
        mm_c += NTC + NTC + NTT
        mm_c += NTT * (NTC + 2 + 1)
        mm_head_end[h] = mm_c

    def head_phase(h):
        ch = [(L["Wq"][dc * 128:(dc + 1) * 128, h * 64:(h + 1) * 64],
               whb[:, dc * 64:(dc + 1) * 64]) for dc in range(DC)]
        ch += [(L["Wupk"][c2 * 128:(c2 + 1) * 128, h * 64:(h + 1) * 64],
                wub[:, c2 * 128:c2 * 128 + 64]) for c2 in range(2)]
        ch += [(L["Wupv"][c2 * 128:(c2 + 1) * 128, h * 64:(h + 1) * 64],
                wub[:, c2 * 128 + 64:(c2 + 1) * 128]) for c2 in range(2)]
        return [(h, ch)], {h: mm_head_end[h - 1] if h else mm0}

    dl_free = [("dve", 0)]
    ngr_free = ("act", 0)
    for h in range(NH):
        ph_, mf_ = head_phase(h)
        cd = _stream_plan(ctx, ph_, mf_, stg)[h]
        pbh = (h % 2) * 64
        chh = h // 2
        dve_head = S.now("dve")
        act_head = S.now("act")
        for tc in range(NTC):
            bk = tc % 4
            w = [("gp", cd), bank_free[bk], ("dve", xn_ready)]
            for dc in range(DC):
                mmc = S.add("tensor", lambda e, tc=tc, dc=dc, bk=bk:
                            e.matmul(
                                ctx.pb[bk][0:64, :],
                                whb[:, dc * 64:(dc + 1) * 64],
                                ctx.AXN[:, dc * T + tc * 512:
                                        dc * T + (tc + 1) * 512],
                                start=(dc == 0), stop=(dc == DC - 1)),
                            waits=(w if dc == 0 else []),
                            inc=("mm" if dc == DC - 1 else None))
            c = S.add("scalar", lambda e, tc=tc, bk=bk: e.activation(
                qh[:, tc * 512:(tc + 1) * 512], ctx.pb[bk][0:64, :],
                AF.Copy), waits=[("mm", mmc), ("act", act_head)],
                inc="act")
            bank_free[bk] = ("act", c)
        for tc in range(NTC):
            bk = tc % 4
            w = [("gp", cd), bank_free[bk], ("act", lat_done)]
            for c2 in range(2):
                mmc = S.add("tensor", lambda e, tc=tc, c2=c2, bk=bk:
                            e.matmul(
                                ctx.pb[bk][0:64, :],
                                wub[:, c2 * 128:c2 * 128 + 64],
                                lat[:, c2 * T + tc * 512:
                                    c2 * T + (tc + 1) * 512],
                                start=(c2 == 0), stop=(c2 == 1)),
                            waits=(w if c2 == 0 else []),
                            inc=("mm" if c2 == 1 else None))
            c = S.add("scalar", lambda e, tc=tc, bk=bk: e.activation(
                krh[:, tc * 512:(tc + 1) * 512], ctx.pb[bk][0:64, :],
                AF.Copy), waits=[("mm", mmc)], inc="act")
            bank_free[bk] = ("act", c)
        for tt in range(NTT):
            bk = tt % 4
            w = [bank_free[bk], ("act", lat_done)]
            for c2 in range(2):
                mmc = S.add("tensor", lambda e, tt=tt, c2=c2, bk=bk:
                            e.matmul(
                                ctx.pb[bk][:, 0:64],
                                lat[:, c2 * T + tt * 128:
                                    c2 * T + (tt + 1) * 128],
                                wub[:, c2 * 128 + 64:(c2 + 1) * 128],
                                start=(c2 == 0), stop=(c2 == 1)),
                            waits=(w if c2 == 0 else []),
                            inc=("mm" if c2 == 1 else None))
            c = S.add("scalar", lambda e, tt=tt, bk=bk: e.activation(
                vh[:, tt * 64:(tt + 1) * 64], ctx.pb[bk][:, 0:64],
                AF.Copy), waits=[("mm", mmc)], inc="act")
            bank_free[bk] = ("act", c)
        prod_act = S.now("act")
        for qt in range(NTT):
            par = 0
            dl = S.add("sync", lambda e, qt=qt, par=par: e.dma_start(
                dk[par], L["decay"][qt * 128:(qt + 1) * 128, :]),
                waits=[dl_free[par]], inc="dma%d" % (2 + par))
            sc_mm = 0
            for kc4 in range(NTC):
                bk = kc4
                sc_mm = S.add("tensor",
                              lambda e, qt=qt, kc4=kc4, bk=bk: e.matmul(
                                  ctx.pb[bk][:, :],
                                  qh[:, qt * 128:(qt + 1) * 128],
                                  krh[:, kc4 * 512:(kc4 + 1) * 512],
                                  start=True, stop=True),
                              waits=[bank_free[bk], ("act", prod_act)],
                              inc="mm")
            cl = 0
            for kc4 in range(NTC):
                S.add("vector", lambda e, kc4=kc4: e.tensor_scalar(
                    sc[:, kc4 * 512:(kc4 + 1) * 512], ctx.pb[kc4][:, :],
                    1.0 / 32.0, 10000.0, ALU.mult, ALU.min),
                    waits=[("mm", sc_mm)])
                cl = S.add("vector", lambda e, kc4=kc4, par=par:
                           e.scalar_tensor_tensor(
                               sc[:, kc4 * 512:(kc4 + 1) * 512],
                               sc[:, kc4 * 512:(kc4 + 1) * 512],
                               -10000.0,
                               dk[par][:, kc4 * 512:(kc4 + 1) * 512],
                               ALU.max, ALU.mult),
                           waits=[("dma%d" % (2 + par), dl)], inc="dve")
            for kc4 in range(NTC):
                bank_free[kc4] = ("dve", cl)
            S.add("vector", lambda e: e.tensor_reduce(rm, sc, AX_.X,
                                                      ALU.max), inc="dve")
            S.selfflush("vector")
            ng = S.add("vector", lambda e: e.tensor_scalar(
                ngr, rm, -1.0, None, ALU.mult), waits=[ngr_free],
                inc="dve")
            S.selfflush("vector")
            pe_ = S.add("scalar", lambda e: e.activation(
                P, sc, AF.Exp, bias=ngr), waits=[("dve", ng)], inc="act")
            ngr_free = ("act", pe_)
            S.add("vector", lambda e: e.tensor_reduce(
                rm, P, AX_.X, ALU.add), waits=[("act", pe_)], inc="dve")
            S.selfflush("vector")
            rv = S.add("vector", lambda e: e.reciprocal(ngr, rm),
                       inc="dve")
            S.selfflush("vector")
            p2 = S.add("vector", lambda e: e.tensor_scalar(
                P2, P, ngr, None, ALU.mult), inc="dve")
            dl_free[par] = ("dve", p2)
            tcopy = []
            for r in range(2):
                bkt = 4 + r
                for kk in range(8):
                    kc = r * 8 + kk
                    w = ([("dve", p2), bank_free[bkt]] if kk == 0 else [])
                    S.add("tensor",
                          lambda e, kc=kc, kk=kk, bkt=bkt: e.transpose(
                              ctx.pbB[bkt][:, kk * 128:(kk + 1) * 128],
                              P2[:, kc * 128:(kc + 1) * 128],
                              cstb(ctx, "ident")), waits=w)
                mmt = S.add("tensor", lambda e: e.matmul(
                    ctx.pb[6][0:1, 0:8], cstb(ctx, "ones_col"), P2[:, 0:8],
                    start=True, stop=True), inc="mm")
                c = S.add("vector", lambda e, r=r, bkt=bkt: e.tensor_copy(
                    PT[:, r * 1024:(r + 1) * 1024],
                    ctx.pbB[bkt][0:128, 0:1024]),
                    waits=[("mm", mmt)], inc="dve")
                bank_free[bkt] = ("dve", c)
                tcopy.append(c)
            avw = [("dve", tcopy[-1]), bank_free[7]]
            mma = 0
            for kc in range(NTT):
                mma = S.add("tensor", lambda e, kc=kc: e.matmul(
                    ctx.pb[7][0:64, 0:128],
                    vh[:, kc * 64:(kc + 1) * 64],
                    PT[:, kc * 128:(kc + 1) * 128],
                    start=(kc == 0), stop=(kc == NTT - 1)),
                    waits=(avw if kc == 0 else []),
                    inc=("mm" if kc == NTT - 1 else None))
            ca = S.add("scalar", lambda e, pbh=pbh, chh=chh, qt=qt:
                       e.activation(
                           ctx_a[pbh:pbh + 64, chh * T + qt * 128:
                                 chh * T + (qt + 1) * 128],
                           ctx.pb[7][0:64, 0:128], AF.Copy),
                       waits=[("mm", mma)], inc="act")
            bank_free[7] = ("act", ca)
        assert S.now("mm") == mm_head_end[h], (h, S.now("mm"),
                                               mm_head_end[h])

    S.add("sync", lambda e: None,
          waits=[("mm", S.now("mm")), ("act", S.now("act")),
                 ("dve", S.now("dve"))])
    hv3 = cv.fork(lnbase, lnbase + 22528)
    stg2 = [hv3.f32(512) for _ in range(2)]
    wob = hv3.bf16(8 * 1024)
    woch = []
    for dc in range(DC):
        for hf in range(2):
            woch.append((L["Wo"][dc * 128:(dc + 1) * 128,
                                 hf * 512:(hf + 1) * 512],
                         wob[:, dc * 1024 + hf * 512:
                             dc * 1024 + (hf + 1) * 512]))
    cast_wo = _stream_plan(ctx, [("wo", woch)], {"wo": S.now("mm")}, stg2)
    cd = cast_wo["wo"]
    for dco in range(DC):
        for tc in range(NTC):
            bk = (dco * NTC + tc) % 4
            w = [("gp", cd), bank_free[bk], ("act", S.now("act"))]
            for dcw in range(DC):
                mmc = S.add("tensor",
                            lambda e, dco=dco, tc=tc, dcw=dcw, bk=bk:
                            e.matmul(
                                ctx.pb[bk][:, :],
                                wob[:, dcw * 1024 + dco * 128:
                                    dcw * 1024 + (dco + 1) * 128],
                                ctx_a[:, dcw * T + tc * 512:
                                      dcw * T + (tc + 1) * 512],
                                start=(dcw == 0), stop=(dcw == DC - 1)),
                            waits=(w if dcw == 0 else []),
                            inc=("mm" if dcw == DC - 1 else None))
            c = S.add("vector", lambda e, dco=dco, tc=tc, bk=bk:
                      e.scalar_tensor_tensor(
                          ctx.xc(dco, tc), ctx.pb[bk][:, :], 1.0,
                          ctx.xc(dco, tc), ALU.mult, ALU.add),
                      waits=[("mm", mmc)], inc="dve")
            bank_free[bk] = ("dve", c)




def emit_a2(ctx, cv, L):
    S = ctx.S
    NH = 16
    lnbase = cv.off
    scr = make_ln_scratch(cv)
    xn_ready = emit_spine_ln(ctx, cv, scr=scr)

    ctx_a = cv.bf16(DC * T)
    ropec = cv.bf16(T, parts=64)
    ropes = cv.bf16(T, parts=64)
    stg = [cv.f32(64) for _ in range(2)]
    PT = cv.bf16(T)
    phmark = cv.off
    qraw = cv.bf16(T, parts=64)
    qperm = cv.bf16(T, parts=64)
    kraw = cv.bf16(T, parts=64)
    kperm = cv.bf16(T, parts=64)
    qr = cv.bf16(T, parts=64)
    kr = cv.bf16(T, parts=64)
    vh = cv.bf16(NTT * 64)
    whb = cv.bf16(8 * 64 * 5)
    tA = cv.f32(512, parts=64)
    tB = cv.f32(512, parts=64)
    bcols = cv.f32(4 * NH, parts=64)
    bocol = cv.f32(DC)
    av1 = cv.fork(lnbase, lnbase + 22528)
    sc = av1.f32(512)
    P = av1.bf16(T)
    P2 = av1.bf16(T)
    cmask = av1.bf16(4 * 512)
    rows = av1.f32(16)
    bvr_f = av1.f32(1024, parts=1)
    bvr = av1.bf16(1024, parts=1)
    rm = rows[:, 0:1]
    ngr = rows[:, 1:2]
    cmx = rows[:, 2:8]

    hole_guard = [("dve", xn_ready), ("mm", S.now("mm")),
                  ("act", S.now("act"))]
    S.add("sync", lambda e: e.dma_start(ropec, L["p_cos"][:, :]),
          inc="dma")
    S.add("sync", lambda e: e.dma_start(ropes, L["p_sin"][:, :]), inc="dma")
    S.add("sync", lambda e: e.dma_start(bcols, L["p_bcols"][:, :]),
          inc="dma")
    S.add("sync", lambda e: e.dma_start(bocol[:, :], L["p_bocol"][:, :]),
          inc="dma")
    S.add("sync", lambda e: e.dma_start(cmask, L["p_cmask"][:, :]),
          waits=hole_guard, inc="dma")
    S.add("sync", lambda e: e.dma_start(bvr_f, L["p_bvrow"][:, :]),
          inc="dma")
    dsm = S.now("dma")
    S.add("sync", lambda e: None, waits=[("dma", dsm)])
    bvc = S.add("vector", lambda e: e.tensor_copy(bvr, bvr_f),
                waits=[("dma", dsm), ("dve", xn_ready)], inc="dve")

    mm0 = S.now("mm")
    mm_head_end = {}
    mm_c = mm0
    per_qt = []
    for qt in range(NTT):
        nkc = qt // 4 + 1
        trr = (nkc * 4 + 7) // 8
        per_qt.append((nkc, trr))
    att_incs = sum(nkc + trr + 1 for nkc, trr in per_qt)
    for h in range(NH):
        mm_c += 4 * NTC + NTT + att_incs
        mm_head_end[h] = mm_c

    def head_phase(h):
        ch = []
        for j, W in enumerate(("WqT", "WqpT", "WkT", "WkpT", "WvT")):
            for dc in range(DC):
                ch.append((L[W][dc * 128:(dc + 1) * 128,
                               h * 64:(h + 1) * 64],
                           whb[:, (j * DC + dc) * 64:
                               (j * DC + dc + 1) * 64]))
        return [(h, ch)], {h: mm_head_end[h - 1] if h else mm0}

    bank_free = {i: ("dve", 0) for i in range(8)}
    sc_free = ("act", 0)
    ngr_free = ("act", 0)
    for h in range(NH):
        ph_, mf_ = head_phase(h)
        cd = _stream_plan(ctx, ph_, mf_, stg)[h]
        pbh = (h % 2) * 64
        chh = h // 2
        act_head = S.now("act")
        for j, tgt in ((0, qraw), (1, qperm), (2, kraw), (3, kperm)):
            for tc in range(NTC):
                bk = tc % 4
                w = [("gp", cd), bank_free[bk], ("dve", xn_ready)]
                for dc in range(DC):
                    mmc = S.add(
                        "tensor", lambda e, j=j, tc=tc, dc=dc, bk=bk:
                        e.matmul(
                            ctx.pb[bk][0:64, :],
                            whb[:, (j * DC + dc) * 64:
                                (j * DC + dc + 1) * 64],
                            ctx.AXN[:, dc * T + tc * 512:
                                    dc * T + (tc + 1) * 512],
                            start=(dc == 0), stop=(dc == DC - 1)),
                        waits=(w if dc == 0 else []),
                        inc=("mm" if dc == DC - 1 else None))
                c = S.add("scalar",
                          lambda e, tgt=tgt, tc=tc, bk=bk, h=h, j=j:
                          e.activation(
                              tgt[:, tc * 512:(tc + 1) * 512],
                              ctx.pb[bk][0:64, :], AF.Identity,
                              bias=bcols[:, j * NH + h:j * NH + h + 1]),
                          waits=[("mm", mmc), ("act", act_head),
                                 ("dma", dsm)], inc="act")
                bank_free[bk] = ("act", c)
        for tt in range(NTT):
            bk = tt % 4
            w = [("gp", cd), bank_free[bk], ("dve", xn_ready)]
            for dc in range(DC):
                S.add("tensor", lambda e, tt=tt, dc=dc, bk=bk: e.matmul(
                    ctx.pb[bk][:, 0:64],
                    ctx.AXN[:, dc * T + tt * 128: dc * T + (tt + 1) * 128],
                    whb[:, (4 * DC + dc) * 64:(4 * DC + dc + 1) * 64],
                    start=(dc == 0), stop=False),
                    waits=(w if dc == 0 else []))
            mmc = S.add("tensor", lambda e, h=h, bk=bk: e.matmul(
                ctx.pb[bk][:, 0:64], cstb(ctx, "ones_row", 1),
                bvr[0:1, h * 64:(h + 1) * 64],
                start=False, stop=True), waits=[("dve", bvc)], inc="mm")
            c = S.add("scalar", lambda e, tt=tt, bk=bk: e.activation(
                vh[:, tt * 64:(tt + 1) * 64], ctx.pb[bk][:, 0:64],
                AF.Copy), waits=[("mm", mmc)], inc="act")
            bank_free[bk] = ("act", c)
        prod_act = S.now("act")
        rp = 0
        for src_r, src_p, dst in ((qraw, qperm, qr), (kraw, kperm, kr)):
            for tc in range(NTC):
                sl = slice(tc * 512, (tc + 1) * 512)
                S.add("vector", lambda e, src_r=src_r, sl=sl:
                      e.tensor_tensor(tA, src_r[:, sl], ropec[:, sl],
                                      ALU.mult), waits=[("act", prod_act)])
                S.add("vector", lambda e, src_p=src_p, sl=sl:
                      e.tensor_tensor(tB, src_p[:, sl], ropes[:, sl],
                                      ALU.mult))
                rp = S.add("vector", lambda e, dst=dst, sl=sl:
                           e.tensor_tensor(dst[:, sl], tA, tB, ALU.add),
                           inc="dve")
        for qt in range(NTT):
            nkc, trr = per_qt[qt]
            khi = nkc - 1
            sc_mm = {}
            for kc4 in range(nkc):
                bk = kc4 % 4
                sc_mm[kc4] = S.add(
                    "tensor", lambda e, qt=qt, kc4=kc4, bk=bk: e.matmul(
                        ctx.pb[bk][:, :],
                        qr[:, qt * 128:(qt + 1) * 128],
                        kr[:, kc4 * 512:(kc4 + 1) * 512],
                        start=True, stop=True),
                    waits=[bank_free[bk], ("dve", rp)], inc="mm")
            dg = S.add("vector", lambda e, qt=qt, khi=khi: e.tensor_tensor(
                sc, ctx.pb[khi % 4][:, :],
                cmask[:, (qt % 4) * 512:(qt % 4 + 1) * 512], ALU.add),
                waits=[("mm", sc_mm[khi]), sc_free], inc="dve")
            for kc4 in range(nkc - 1):
                S.add("vector", lambda e, kc4=kc4: e.tensor_reduce(
                    cmx[:, kc4:kc4 + 1], ctx.pb[kc4 % 4][:, :], AX_.X,
                    ALU.max), waits=[("mm", sc_mm[kc4])])
            S.add("vector", lambda e, khi=khi: e.tensor_reduce(
                cmx[:, khi:khi + 1], sc, AX_.X, ALU.max), inc="dve")
            S.selfflush("vector")
            S.add("vector", lambda e, nkc=nkc: e.tensor_reduce(
                rm, cmx[:, 0:nkc], AX_.X, ALU.max), inc="dve")
            S.selfflush("vector")
            ng = S.add("vector", lambda e: e.tensor_scalar(
                ngr, rm, -1.0 / 8.0, None, ALU.mult),
                waits=[ngr_free], inc="dve")
            S.selfflush("vector")
            pe_ = 0
            for kc4 in range(nkc):
                srcap = sc if kc4 == khi else ctx.pb[kc4 % 4][:, :]
                pe_ = S.add("scalar", lambda e, kc4=kc4, srcap=srcap:
                            e.activation(
                                P[:, kc4 * 512:(kc4 + 1) * 512], srcap,
                                AF.Exp, bias=ngr, scale=1.0 / 8.0),
                            waits=[("dve", ng)], inc="act")
            for kc4 in range(nkc):
                bank_free[kc4 % 4] = ("act", pe_)
            sc_free = ("act", pe_)
            ngr_free = ("act", pe_)
            S.add("vector", lambda e, nkc=nkc: e.tensor_reduce(
                rm, P[:, 0:nkc * 512], AX_.X, ALU.add),
                waits=[("act", pe_)], inc="dve")
            S.selfflush("vector")
            rv = S.add("vector", lambda e: e.reciprocal(ngr, rm),
                       inc="dve")
            S.selfflush("vector")
            p2 = S.add("vector", lambda e, nkc=nkc: e.tensor_scalar(
                P2[:, 0:nkc * 512], P[:, 0:nkc * 512], ngr, None,
                ALU.mult), inc="dve")
            ntile = nkc * 4
            tlast = 0
            for r in range(trr):
                bkt = 4 + (r % 2)
                n_in_r = min(8, ntile - r * 8)
                for kk in range(n_in_r):
                    kc = r * 8 + kk
                    w = ([("dve", p2), bank_free[bkt]] if kk == 0 else [])
                    S.add("tensor",
                          lambda e, kc=kc, kk=kk, bkt=bkt: e.transpose(
                              ctx.pbB[bkt][:, kk * 128:(kk + 1) * 128],
                              P2[:, kc * 128:(kc + 1) * 128],
                              cstb(ctx, "ident")), waits=w)
                mmt = S.add("tensor", lambda e: e.matmul(
                    ctx.pb[6][0:1, 0:8], cstb(ctx, "ones_col"), P2[:, 0:8],
                    start=True, stop=True), inc="mm")
                tlast = S.add("vector",
                              lambda e, r=r, bkt=bkt, n_in_r=n_in_r:
                              e.tensor_copy(
                                  PT[:, r * 1024:r * 1024 + n_in_r * 128],
                                  ctx.pbB[bkt][0:128, 0:n_in_r * 128]),
                              waits=[("mm", mmt)], inc="dve")
                bank_free[bkt] = ("dve", tlast)
            avw = [("dve", tlast), bank_free[7]]
            mma = 0
            for kc in range(ntile):
                mma = S.add("tensor", lambda e, kc=kc, ntile=ntile: e.matmul(
                    ctx.pb[7][0:64, 0:128],
                    vh[:, kc * 64:(kc + 1) * 64],
                    PT[:, kc * 128:(kc + 1) * 128],
                    start=(kc == 0), stop=(kc == ntile - 1)),
                    waits=(avw if kc == 0 else []),
                    inc=("mm" if kc == ntile - 1 else None))
            ca = S.add("scalar", lambda e, pbh=pbh, chh=chh, qt=qt:
                       e.activation(
                           ctx_a[pbh:pbh + 64, chh * T + qt * 128:
                                 chh * T + (qt + 1) * 128],
                           ctx.pb[7][0:64, 0:128], AF.Copy),
                       waits=[("mm", mma)], inc="act")
            bank_free[7] = ("act", ca)
        assert S.now("mm") == mm_head_end[h], (h, S.now("mm"),
                                               mm_head_end[h])

    ctx.dbg_a2 = {"ctx_a": ctx_a}
    S.add("sync", lambda e: None,
          waits=[("mm", S.now("mm")), ("act", S.now("act")),
                 ("dve", S.now("dve"))])
    hv3 = cv.fork(phmark)
    stg2 = [hv3.f32(512) for _ in range(2)]
    wob = hv3.bf16(8 * 1024)
    woch = []
    for dc in range(DC):
        for hf in range(2):
            woch.append((L["WoT"][dc * 128:(dc + 1) * 128,
                                  hf * 512:(hf + 1) * 512],
                         wob[:, dc * 1024 + hf * 512:
                             dc * 1024 + (hf + 1) * 512]))
    cast_wo = _stream_plan(ctx, [("wo", woch)], {"wo": S.now("mm")}, stg2)
    cd = cast_wo["wo"]
    t1b = hv3.f32(512)
    bo2 = hv3.f32(DC)
    S.add("sync", lambda e: e.dma_start(bo2[:, :], L["p_bocol"][:, :]),
          inc="dma")
    dbo = S.now("dma")
    for dco in range(DC):
        for tc in range(NTC):
            bk = (dco * NTC + tc) % 4
            w = [("gp", cd), bank_free[bk], ("act", S.now("act"))]
            for dcw in range(DC):
                mmc = S.add("tensor",
                            lambda e, dco=dco, tc=tc, dcw=dcw, bk=bk:
                            e.matmul(
                                ctx.pb[bk][:, :],
                                wob[:, dcw * 1024 + dco * 128:
                                    dcw * 1024 + (dco + 1) * 128],
                                ctx_a[:, dcw * T + tc * 512:
                                      dcw * T + (tc + 1) * 512],
                                start=(dcw == 0), stop=(dcw == DC - 1)),
                            waits=(w if dcw == 0 else []),
                            inc=("mm" if dcw == DC - 1 else None))
            c = S.add("vector", lambda e, dco=dco, bk=bk: e.tensor_scalar(
                t1b, ctx.pb[bk][:, :], bo2[:, dco:dco + 1], None,
                ALU.add), waits=[("mm", mmc), ("dma", dbo)], inc="dve")
            S.add("vector", lambda e, dco=dco, tc=tc: e.tensor_tensor(
                ctx.xc(dco, tc), ctx.xc(dco, tc), t1b, ALU.add))
            bank_free[bk] = ("dve", c)




A0_BLOCKS = []
for _i in range(0, T, 256):
    _bs = max(0, _i - 128)
    _be = min(T, _i + 384)
    A0_BLOCKS.append((_i, _bs, _be))


def emit_a0(ctx, cv, L):
    S = ctx.S
    NH = 16
    lnbase = cv.off
    scr = make_ln_scratch(cv)
    xn_ready = emit_spine_ln(ctx, cv, scr=scr)

    h1 = cv.bf16(DC * T)
    ctxmark = cv.off
    ctx_a = cv.bf16(DC * T)
    vblkmark = cv.off
    vblk = cv.bf16(4 * 1024)
    stg = [cv.f32(512) for _ in range(2)]
    wcb = [cv.bf16(1024) for _ in range(2)]
    cols = cv.f32(64)
    colsb = cv.f32(40)
    t1 = cv.f32(512)
    av1 = cv.fork(lnbase, lnbase + 22528)
    qblk = av1.bf16(8 * 512)
    kblk = av1.bf16(8 * 512)
    P = av1.bf16(512)
    P2 = av1.bf16(512)
    PT = av1.bf16(512)
    abt = av1.bf16(512)
    bvr = av1.bf16(1024, parts=1)
    rows = cv.f32(2)
    rm = rows[:, 0:1]
    ngr = rows[:, 1:2]

    S.add("sync", lambda e: e.dma_start(cols, L["p_cols"][:, :]), inc="dma")
    S.add("sync", lambda e: e.dma_start(colsb, L["p_colsb"][:, :]),
          inc="dma")
    dsm0 = S.now("dma")
    S.add("sync", lambda e: None, waits=[("dma", dsm0)])

    h1_ready = emit_spine_ln(ctx, cv, src=ctx.AXN, dst=h1,
                             gcol=cols[:, 0:8], bcol=cols[:, 8:16], scr=scr)

    hole_guard = [("dve", h1_ready), ("mm", S.now("mm")),
                  ("act", S.now("act"))]
    S.add("sync", lambda e: e.dma_start(abt, L["p_ab"][:, :]),
          waits=hole_guard, inc="dma")
    S.add("sync", lambda e: e.dma_start(stg[0][0:1, :],
                                        L["p_bvrow"][:, 0:512]), inc="dma")
    S.add("sync", lambda e: e.dma_start(stg[1][0:1, :],
                                        L["p_bvrow"][:, 512:1024]),
          inc="dma")
    dsm = S.now("dma")
    S.add("sync", lambda e: None, waits=[("dma", dsm)])
    S.add("vector", lambda e: e.tensor_copy(bvr[0:1, 0:512],
                                            stg[0][0:1, :]),
          waits=[("dma", dsm), ("dve", h1_ready)])
    bvc = S.add("vector", lambda e: e.tensor_copy(bvr[0:1, 512:1024],
                                                  stg[1][0:1, :]),
                inc="dve")

    bank_free = {i: ("dve", 0) for i in range(8)}
    ngr_free = ("act", 0)
    ctx_done = []
    t1_free = ("act", 0)
    for b, (i0, bs, be) in enumerate(A0_BLOCKS):
        W = be - bs
        nqt = W // 128
        prods = (("Wq", 32, qblk, "fm"), ("Wk", 40, kblk, "fm"),
                 ("Wv", 48, vblk, "tm"))
        prev_ctx_done = list(ctx_done)
        for pj, (wname, bcol0, tgt, kind) in enumerate(prods):
            gp_base = S.now("gp")
            mm_uses = {}
            g = 0
            for dc in range(DC):
                cast_c = 0
                for hf in range(2):
                    par = g % 2
                    psem = "dma%d" % par
                    w = []
                    if g >= 2:
                        w.append(("gp", gp_base + g - 1))
                    if dc >= 2 and hf == 0:
                        w.append(("mm", mm_uses[dc - 2]))
                    dcnt = S.add("sync", lambda e, wname=wname, dc=dc,
                                 hf=hf, par=par: e.dma_start(
                                     stg[par],
                                     L[wname][dc * 128:(dc + 1) * 128,
                                              hf * 512:(hf + 1) * 512]),
                                 waits=w, inc=psem)
                    cast_c = S.add(
                        "gpsimd", lambda e, dc=dc, hf=hf, par=par:
                        e.tensor_copy(
                            wcb[dc % 2][:, hf * 512:(hf + 1) * 512],
                            stg[par]), waits=[(psem, dcnt)], inc="gp")
                    g += 1
                if kind == "fm":
                    for oc in range(DC):
                        bk = oc
                        w3 = [("gp", cast_c)]
                        if dc == 0:
                            w3 += [bank_free[bk], ("dve", h1_ready)]
                        mmx = S.add(
                            "tensor",
                            lambda e, dc=dc, oc=oc, bk=bk, bs=bs, W=W:
                            e.matmul(
                                ctx.pb[bk][:, 0:W],
                                wcb[dc % 2][:, oc * 128:(oc + 1) * 128],
                                h1[:, dc * T + bs: dc * T + bs + W],
                                start=(dc == 0), stop=(dc == DC - 1)),
                            waits=w3, inc=("mm" if oc == DC - 1 else None))
                        if oc == DC - 1:
                            mm_uses[dc] = mmx
                else:
                    for tt in range(nqt):
                        for vc in range(2):
                            bk = tt * 2 + vc
                            w3 = [("gp", cast_c)]
                            if dc == 0:
                                w3 += [bank_free[bk], ("dve", h1_ready)]
                            mmx = S.add(
                                "tensor",
                                lambda e, dc=dc, tt=tt, vc=vc, bk=bk,
                                bs=bs: e.matmul(
                                    ctx.pb[bk][:, :],
                                    h1[:, dc * T + bs + tt * 128:
                                       dc * T + bs + (tt + 1) * 128],
                                    wcb[dc % 2][:, vc * 512:
                                                (vc + 1) * 512],
                                    start=(dc == 0), stop=False),
                                waits=w3,
                                inc=("mm" if (tt, vc) == (nqt - 1, 1)
                                     else None))
                            if (tt, vc) == (nqt - 1, 1):
                                mm_uses[dc] = mmx
            if kind == "fm":
                for oc in range(DC):
                    c = S.add("scalar",
                              lambda e, oc=oc, tgt=tgt, bcol0=bcol0, W=W:
                              e.activation(
                                  tgt[:, oc * 512:oc * 512 + W],
                                  ctx.pb[oc][:, 0:W], AF.Identity,
                                  bias=cols[:, bcol0 + oc:
                                            bcol0 + oc + 1]),
                              waits=[("mm", mm_uses[DC - 1]),
                                     ("dma", dsm)] + prev_ctx_done,
                              inc="act")
                    bank_free[oc] = ("act", c)
            else:
                for tt in range(nqt):
                    for vc in range(2):
                        bk = tt * 2 + vc
                        mmb = S.add("tensor",
                                    lambda e, vc=vc, bk=bk: e.matmul(
                                        ctx.pb[bk][:, :],
                                        cstb(ctx, "ones_row", 1),
                                        bvr[0:1, vc * 512:(vc + 1) * 512],
                                        start=False, stop=True),
                                    waits=[("dve", bvc)], inc="mm")
                        c = S.add("scalar",
                                  lambda e, tt=tt, vc=vc, bk=bk:
                                  e.activation(
                                      vblk[:, tt * 1024 + vc * 512:
                                           tt * 1024 + (vc + 1) * 512],
                                      ctx.pb[bk][:, :], AF.Copy),
                                  waits=[("mm", mmb)], inc="act")
                        bank_free[bk] = ("act", c)
        prod_act = S.now("act")

        for h in range(NH):
            pbh = (h % 2) * 64
            chh = h // 2
            for qt in range(nqt):
                bk = qt % 4
                smm = S.add("tensor",
                            lambda e, pbh=pbh, chh=chh, qt=qt, bk=bk, W=W:
                            e.matmul(
                                ctx.pb[bk][:, 0:W],
                                qblk[pbh:pbh + 64,
                                     chh * 512 + qt * 128:
                                     chh * 512 + (qt + 1) * 128],
                                kblk[pbh:pbh + 64,
                                     chh * 512:chh * 512 + W],
                                start=True, stop=True),
                            waits=[bank_free[bk], ("act", prod_act)],
                            inc="mm")
                S.add("vector", lambda e, bk=bk, W=W: e.tensor_reduce(
                    rm, ctx.pb[bk][:, 0:W], AX_.X, ALU.max),
                    waits=[("mm", smm)], inc="dve")
                S.selfflush("vector")
                ng = S.add("vector", lambda e: e.tensor_scalar(
                    ngr, rm, -0.125, None, ALU.mult), waits=[ngr_free],
                    inc="dve")
                S.selfflush("vector")
                pe_ = S.add("scalar", lambda e, bk=bk, W=W: e.activation(
                    P[:, 0:W], ctx.pb[bk][:, 0:W], AF.Exp, bias=ngr,
                    scale=0.125), waits=[("dve", ng)], inc="act")
                ngr_free = ("act", pe_)
                bank_free[bk] = ("act", pe_)
                S.add("vector", lambda e, W=W: e.tensor_reduce(
                    rm, P[:, 0:W], AX_.X, ALU.add),
                    waits=[("act", pe_)], inc="dve")
                S.selfflush("vector")
                S.add("vector", lambda e: e.reciprocal(ngr, rm), inc="dve")
                S.selfflush("vector")
                p2 = S.add("vector", lambda e, W=W: e.tensor_scalar(
                    P2[:, 0:W], P[:, 0:W], ngr, None, ALU.mult),
                    inc="dve")
                ntl = W // 128
                bkt = 4 + (qt % 2)
                for kk in range(ntl):
                    w = ([("dve", p2), bank_free[bkt]] if kk == 0 else [])
                    S.add("tensor", lambda e, kk=kk, bkt=bkt: e.transpose(
                        ctx.pbB[bkt][:, kk * 128:(kk + 1) * 128],
                        P2[:, kk * 128:(kk + 1) * 128],
                        cstb(ctx, "ident")), waits=w)
                mmt = S.add("tensor", lambda e: e.matmul(
                    ctx.pb[6][0:1, 0:8], cstb(ctx, "ones_col"),
                    P2[:, 0:8], start=True, stop=True), inc="mm")
                tl = S.add("vector", lambda e, bkt=bkt, ntl=ntl:
                           e.tensor_copy(PT[:, 0:ntl * 128],
                                         ctx.pbB[bkt][0:128,
                                                      0:ntl * 128]),
                           waits=[("mm", mmt)], inc="dve")
                bank_free[bkt] = ("dve", tl)
                avw = [("dve", tl), bank_free[7]]
                mma = 0
                for kc in range(ntl):
                    mma = S.add("tensor",
                                lambda e, kc=kc, h=h, ntl=ntl: e.matmul(
                                    ctx.pb[7][0:64, 0:128],
                                    vblk[:, kc * 1024 + h * 64:
                                         kc * 1024 + (h + 1) * 64],
                                    PT[:, kc * 128:(kc + 1) * 128],
                                    start=(kc == 0),
                                    stop=(kc == ntl - 1)),
                                waits=(avw if kc == 0 else []),
                                inc=("mm" if kc == ntl - 1 else None))
                g0 = bs + qt * 128
                csl = ctx_a[pbh:pbh + 64,
                            chh * T + g0: chh * T + g0 + 128]
                if b == 0 or qt >= 2:
                    c = S.add("scalar", lambda e, csl=csl: e.activation(
                        csl, ctx.pb[7][0:64, 0:128], AF.Copy),
                        waits=[("mm", mma)] + prev_ctx_done, inc="act")
                    bank_free[7] = ("act", c)
                else:
                    asl = abt[pbh:pbh + 64, qt * 128:(qt + 1) * 128]
                    osl = abt[pbh:pbh + 64, 256 + qt * 128:
                              256 + (qt + 1) * 128]
                    tb1 = t1[0:64, 0:128]
                    tb2 = t1[0:64, 128:256]
                    S.add("vector", lambda e, asl=asl, tb1=tb1:
                          e.tensor_tensor(tb1, ctx.pb[7][0:64, 0:128],
                                          asl, ALU.mult),
                          waits=[("mm", mma), t1_free] + prev_ctx_done)
                    S.add("vector", lambda e, csl=csl, osl=osl, tb2=tb2:
                          e.tensor_tensor(tb2, csl, osl, ALU.mult))
                    c = S.add("vector",
                              lambda e, csl=csl, tb1=tb1, tb2=tb2:
                              e.tensor_tensor(csl, tb1, tb2, ALU.add),
                              inc="dve")
                    bank_free[7] = ("dve", c)
        ctx_done = [("act", S.now("act")), ("dve", S.now("dve"))]

    S.add("sync", lambda e: None,
          waits=[("mm", S.now("mm")), ("act", S.now("act")),
                 ("dve", S.now("dve"))])
    hv3 = cv.fork(lnbase, lnbase + 22528)
    stg2 = [hv3.f32(512) for _ in range(2)]
    wpb = hv3.bf16(8 * 1024)
    wpch = []
    for dc in range(DC):
        for hf in range(2):
            wpch.append((L["Wproj"][dc * 128:(dc + 1) * 128,
                                    hf * 512:(hf + 1) * 512],
                         wpb[:, dc * 1024 + hf * 512:
                             dc * 1024 + (hf + 1) * 512]))
    cast_wp = _stream_plan(ctx, [("wp", wpch)], {"wp": S.now("mm")}, stg2)
    cdp = cast_wp["wp"]
    for dco in range(DC):
        for tc in range(NTC):
            bk = (dco * NTC + tc) % 4
            w = [("gp", cdp), bank_free[bk]]
            for dcw in range(DC):
                mmc = S.add("tensor",
                            lambda e, dco=dco, tc=tc, dcw=dcw, bk=bk:
                            e.matmul(
                                ctx.pb[bk][:, :],
                                wpb[:, dcw * 1024 + dco * 128:
                                    dcw * 1024 + (dco + 1) * 128],
                                ctx_a[:, dcw * T + tc * 512:
                                      dcw * T + (tc + 1) * 512],
                                start=(dcw == 0), stop=(dcw == DC - 1)),
                            waits=(w if dcw == 0 else []),
                            inc=("mm" if dcw == DC - 1 else None))
            c1 = S.add("vector", lambda e, dco=dco, bk=bk: e.tensor_scalar(
                t1, ctx.pb[bk][:, :], cols[:, 56 + dco:56 + dco + 1],
                None, ALU.add), waits=[("mm", mmc), t1_free], inc="dve")
            c2 = S.add("vector", lambda e, dco=dco, tc=tc: e.tensor_tensor(
                t1, t1, ctx.xnc(dco, tc), ALU.add), inc="dve")
            S.add("vector", lambda e, dco=dco, tc=tc: e.tensor_tensor(
                ctx.xc(dco, tc), ctx.xc(dco, tc), t1, ALU.add))
            oc_ = S.add("scalar", lambda e, dco=dco, tc=tc: e.activation(
                ctx.xnc(dco, tc), t1, AF.Copy), waits=[("dve", c2)],
                inc="act")
            t1_free = ("act", oc_)
            bank_free[bk] = ("act", oc_)

    ln2 = emit_spine_ln(ctx, cv, src=ctx.AXN, dst=h1,
                        gcol=cols[:, 16:24], bcol=cols[:, 24:32], scr=scr)

    fv = cv.fork(ctxmark)
    w1q = fv.bf16(8 * 1024)
    w2q = fv.bf16(8 * 1024)
    fv2 = cv.fork(vblkmark)
    hh = fv2.bf16(8 * 512)
    mm0f = S.now("mm")
    mm_phase_end = {}
    mm_c = mm0f
    for qq in range(4):
        mm_c += 16 * NTC
        mm_phase_end[qq] = mm_c
    phases = []
    mm_free = {}
    prev = None
    for qq in range(4):
        ch = []
        for dc in range(DC):
            for hf in range(2):
                ch.append((L["w1"][dc * 128:(dc + 1) * 128,
                                   qq * 1024 + hf * 512:
                                   qq * 1024 + (hf + 1) * 512],
                           w1q[:, dc * 1024 + hf * 512:
                               dc * 1024 + (hf + 1) * 512]))
        for hc in range(8):
            r0 = qq * 1024 + hc * 128
            for hf in range(2):
                ch.append((L["w2"][r0:r0 + 128, hf * 512:(hf + 1) * 512],
                           w2q[:, hc * 1024 + hf * 512:
                               hc * 1024 + (hf + 1) * 512]))
        mm_free[qq] = mm_phase_end[prev] if prev is not None else \
            S.now("mm")
        prev = qq
        phases.append((qq, ch))
    cast_done = _stream_plan(ctx, phases, mm_free, stg)
    for qq in range(4):
        cd = cast_done[qq]
        for tc in range(NTC):
            for hc in range(8):
                bk = hc % 4
                w = [("gp", cd), bank_free[bk], ("dve", ln2)]
                for dc in range(DC):
                    mmc = S.add(
                        "tensor", lambda e, hc=hc, tc=tc, dc=dc, bk=bk:
                        e.matmul(
                            ctx.pb[bk][:, :],
                            w1q[:, dc * 1024 + hc * 128:
                                dc * 1024 + (hc + 1) * 128],
                            h1[:, dc * T + tc * 512:
                               dc * T + (tc + 1) * 512],
                            start=(dc == 0), stop=(dc == DC - 1)),
                        waits=(w if dc == 0 else []),
                        inc=("mm" if dc == DC - 1 else None))
                c = S.add("scalar", lambda e, hc=hc, bk=bk, qq=qq:
                          e.activation(
                              hh[:, hc * 512:(hc + 1) * 512],
                              ctx.pb[bk][:, :], AF.Gelu,
                              bias=colsb[:, qq * 8 + hc:qq * 8 + hc + 1]),
                          waits=[("mm", mmc)], inc="act")
                bank_free[bk] = ("act", c)
            h_ready = ("act", S.now("act"))
            for dco in range(DC):
                bk = dco % 4
                w = [h_ready, bank_free[bk]]
                for hc in range(8):
                    mmc = S.add(
                        "tensor", lambda e, dco=dco, hc=hc, bk=bk:
                        e.matmul(
                            ctx.pb[bk][:, :],
                            w2q[:, hc * 1024 + dco * 128:
                                hc * 1024 + (dco + 1) * 128],
                            hh[:, hc * 512:(hc + 1) * 512],
                            start=(hc == 0), stop=(hc == 7)),
                        waits=(w if hc == 0 else []),
                        inc=("mm" if hc == 7 else None))
                if qq < 3:
                    c = S.add("vector",
                              lambda e, dco=dco, tc=tc, bk=bk:
                              e.scalar_tensor_tensor(
                                  ctx.xc(dco, tc), ctx.pb[bk][:, :], 1.0,
                                  ctx.xc(dco, tc), ALU.mult, ALU.add),
                              waits=[("mm", mmc)], inc="dve")
                else:
                    c1 = S.add("vector",
                               lambda e, dco=dco, bk=bk: e.tensor_scalar(
                                   t1, ctx.pb[bk][:, :],
                                   colsb[:, 32 + dco:32 + dco + 1], None,
                                   ALU.add),
                               waits=[("mm", mmc), t1_free], inc="dve")
                    t1_free = ("dve", c1)
                    c = S.add("vector", lambda e, dco=dco, tc=tc:
                              e.tensor_tensor(
                                  ctx.xc(dco, tc), ctx.xc(dco, tc), t1,
                                  ALU.add), inc="dve")
                bank_free[bk] = ("dve", c)
        assert S.now("mm") == mm_phase_end[qq], (qq, S.now("mm"))



def prep_shared(inputs):
    _init_consts()
    sh = {"cst_b16": _CSTB_ARR, "cst_f32": _CSTF_ARR}
    Wr = np.asarray(inputs["L0_ffn_Wr"])
    sh["L0f_p_Wr"] = np.ascontiguousarray(
        Wr.reshape(8, 128, 8).transpose(1, 0, 2).reshape(128, 64))
    b1 = np.asarray(inputs["L0_ffn_b1"])
    sh["L0f_p_b1fm"] = np.ascontiguousarray(
        b1.reshape(8, 16, 128).transpose(2, 0, 1).reshape(128, 128))
    sh["L0f_W1"] = np.asarray(inputs["L0_ffn_W1"])
    sh["L0f_W2"] = np.asarray(inputs["L0_ffn_W2"])
    sh["L0f_b2"] = np.asarray(inputs["L0_ffn_b2"])
    sh["L0f_br"] = np.asarray(inputs["L0_ffn_br"]).reshape(1, 8)
    Wg = np.asarray(inputs["L1_ffn_Wg"])
    sh["L1f_p_Wg"] = np.ascontiguousarray(
        Wg.reshape(8, 128, 8).transpose(1, 0, 2).reshape(128, 64))
    for n in ("Ws1", "Ws3", "Ws2", "We1", "We3", "We2"):
        sh["L1f_" + n] = np.asarray(inputs["L1_ffn_" + n])
    sh["L1a_W1"] = np.asarray(inputs["L1_attn_W1"])
    sh["L1a_W2"] = np.asarray(inputs["L1_attn_W2"])
    b1a = np.asarray(inputs["L1_attn_b1"])
    b1h = np.zeros((64, 64), np.float32)
    for h in range(16):
        for j in range(4):
            b1h[:, h * 4 + j] = b1a[j * 1024 + h * 64: j * 1024 + (h + 1) * 64]
    sh["L1a_p_b1h"] = b1h
    sh["L1a_p_b1vrow"] = np.ascontiguousarray(b1a[1024:2048].reshape(1, 1024))
    b2a = np.asarray(inputs["L1_attn_b2"]).reshape(8, 128).T
    ga = np.asarray(inputs["L1_attn_gamma"]).reshape(8, 128).T
    ba = np.asarray(inputs["L1_attn_beta"]).reshape(8, 128).T
    sh["L1a_p_cols"] = np.ascontiguousarray(np.concatenate([b2a, ga, ba], 1))
    for n in ("wq", "wk", "wv", "wproj", "w1", "w2"):
        sh["L0a_" + n] = np.asarray(inputs["L0_attn_" + n])

    def fmcols(v):
        return np.asarray(v).reshape(-1, 128).T
    sh["L0a_p_cols"] = np.ascontiguousarray(np.concatenate(
        [fmcols(inputs["L0_attn_ln1_g"]), fmcols(inputs["L0_attn_ln1_b"]),
         fmcols(inputs["L0_attn_ln2_g"]), fmcols(inputs["L0_attn_ln2_b"]),
         fmcols(inputs["L0_attn_bq"]), fmcols(inputs["L0_attn_bk"]),
         fmcols(inputs["L0_attn_bv"]), fmcols(inputs["L0_attn_bproj"])],
        axis=1)).astype(np.float32)
    sh["L0a_p_colsb"] = np.ascontiguousarray(np.concatenate(
        [fmcols(inputs["L0_attn_b1"]), fmcols(inputs["L0_attn_b2"])],
        axis=1)).astype(np.float32)
    al = (np.arange(256) / 255.0)
    ab = np.zeros((128, 512))
    ab[:, 0:256] = al[None, :]
    ab[:, 256:512] = 1.0 - al[None, :]
    sh["L0a_p_ab"] = ab.astype(BF)
    sh["L0a_p_bvrow"] = np.ascontiguousarray(
        np.asarray(inputs["L0_attn_bv"]).reshape(1, 1024))
    WqT = np.asarray(inputs["L2_attn_Wq"]).T
    WkT = np.asarray(inputs["L2_attn_Wk"]).T

    def permcols(Wt):
        o = np.empty_like(Wt)
        for hh in range(16):
            o[:, hh * 64:hh * 64 + 32] = Wt[:, hh * 64 + 32:hh * 64 + 64]
            o[:, hh * 64 + 32:hh * 64 + 64] = Wt[:, hh * 64:hh * 64 + 32]
        return o
    sh["L2a_p_WqT"] = np.ascontiguousarray(WqT)
    sh["L2a_p_WqpT"] = np.ascontiguousarray(permcols(WqT))
    sh["L2a_p_WkT"] = np.ascontiguousarray(WkT)
    sh["L2a_p_WkpT"] = np.ascontiguousarray(permcols(WkT))
    sh["L2a_p_WvT"] = np.ascontiguousarray(np.asarray(inputs["L2_attn_Wv"]).T)
    sh["L2a_p_WoT"] = np.ascontiguousarray(np.asarray(inputs["L2_attn_Wo"]).T)
    dd_ = np.arange(64)
    theta = 10000.0 ** (-2.0 * (dd_ % 32) / 64.0)
    pos = np.arange(1, T + 1, dtype=np.float64)
    C = pos[None, :] * theta[:, None]
    cos64 = np.cos(C)
    sin64 = np.sin(C) * np.where(dd_ < 32, -1.0, 1.0)[:, None]
    sh["L2a_p_cos"] = np.ascontiguousarray(cos64).astype(BF)
    sh["L2a_p_sin"] = np.ascontiguousarray(sin64).astype(BF)
    bq = np.asarray(inputs["L2_attn_bq"])
    bk_ = np.asarray(inputs["L2_attn_bk"])

    def headcols(b):
        return b.reshape(16, 64).T

    def permhead(b):
        o = np.empty_like(b)
        o[:, :] = b.reshape(16, 64)[:, list(range(32, 64)) +
                                    list(range(32))].reshape(16, 64) \
            if False else b
        return o
    bqp = bq.reshape(16, 64)[:, list(range(32, 64)) + list(range(32))]
    bkp = bk_.reshape(16, 64)[:, list(range(32, 64)) + list(range(32))]
    sh["L2a_p_bcols"] = np.ascontiguousarray(np.concatenate(
        [headcols(bq), bqp.T.reshape(64, 16) if False else bqp.T,
         headcols(bk_), bkp.T], axis=1)).astype(np.float32)
    sh["L2a_p_bocol"] = np.ascontiguousarray(
        np.asarray(inputs["L2_attn_bo"]).reshape(8, 128).T)
    cm = np.zeros((128, 4 * 512), np.float64)
    for r4 in range(4):
        qpos = np.arange(128)[:, None] + r4 * 128
        kpos = np.arange(512)[None, :]
        cm[:, r4 * 512:(r4 + 1) * 512] = np.where(kpos <= qpos, 0.0, -1e30)
    sh["L2a_p_cmask"] = cm.astype(BF)
    sh["L2a_p_bvrow"] = np.ascontiguousarray(
        np.asarray(inputs["L2_attn_bv"]).reshape(1, 1024))
    sh["L3a_Wq"] = np.asarray(inputs["L3_attn_Wq"])
    sh["L3a_Wk"] = np.asarray(inputs["L3_attn_Wk"])
    sh["L3a_Wv"] = np.asarray(inputs["L3_attn_Wv"])
    sh["L3a_Wo"] = np.asarray(inputs["L3_attn_Wo"])
    sh["L3a_Wdown"] = np.asarray(inputs["L3_attn_W_down_kv"])
    sh["L3a_Wupk"] = np.asarray(inputs["L3_attn_W_up_k"])
    sh["L3a_Wupv"] = np.asarray(inputs["L3_attn_W_up_v"])
    idx = np.arange(2048, dtype=np.float64)
    sh["L3a_p_decay"] = np.exp(
        -1e-4 * np.abs(idx[:, None] - idx[None, :])).astype(BF)
    sh["L2f_p_WfcT"] = np.ascontiguousarray(
        np.asarray(inputs["L2_ffn_W_fc"]).T)
    sh["L2f_p_WprojT"] = np.ascontiguousarray(
        np.asarray(inputs["L2_ffn_W_proj"]).T)
    rw = np.asarray(inputs["L3_ffn_router_w"])
    sh["L3f_p_router"] = np.ascontiguousarray(
        rw.T.reshape(8, 128, 8).transpose(1, 0, 2).reshape(128, 64))
    g = np.asarray(inputs["L3_ffn_gamma"]).reshape(8, 128).T
    b = np.asarray(inputs["L3_ffn_beta"]).reshape(8, 128).T
    sh["L3f_p_gb"] = np.ascontiguousarray(
        np.concatenate([g, b], axis=1))
    sh["L3f_W1"] = np.asarray(inputs["L3_ffn_W1"])
    sh["L3f_W2"] = np.asarray(inputs["L3_ffn_W2"])
    return sh


PART_NAMES = {
    "f0a": {"L0f_p_Wr", "L0f_p_b1fm", "L0f_W1", "L0f_W2", "L0f_b2",
            "L0f_br"},
    "dbgln": set(),
    "f1a": {"L1f_p_Wg", "L1f_Ws1", "L1f_Ws3", "L1f_Ws2", "L1f_We1",
            "L1f_We3", "L1f_We2"},
    "a1": {"L1a_W1", "L1a_W2", "L1a_p_cols", "L1a_p_b1h", "L1a_p_b1vrow"},
    "f2a": {"L2f_p_WfcT", "L2f_p_WprojT"},
    "a3": {"L3a_Wq", "L3a_Wk", "L3a_Wv", "L3a_Wo", "L3a_Wdown",
           "L3a_Wupk", "L3a_Wupv", "L3a_p_decay"},
    "a0": {"L0a_wq", "L0a_wk", "L0a_wv", "L0a_wproj", "L0a_w1", "L0a_w2",
           "L0a_p_cols", "L0a_p_colsb", "L0a_p_ab", "L0a_p_bvrow"},
    "a2": {"L2a_p_WqT", "L2a_p_WqpT", "L2a_p_WkT", "L2a_p_WkpT",
           "L2a_p_WvT", "L2a_p_WoT", "L2a_p_cos", "L2a_p_sin",
           "L2a_p_bcols", "L2a_p_bocol", "L2a_p_cmask", "L2a_p_bvrow"},
    "f3a": {"L3f_p_router", "L3f_p_gb", "L3f_W1", "L3f_W2"},
}
PART_NAMES["f1b"] = PART_NAMES["f1a"]
PART_NAMES["f2b"] = PART_NAMES["f2a"]
PART_NAMES["dbgf1_sh"] = PART_NAMES["f1a"]
PART_NAMES["dbgf1_e0"] = PART_NAMES["f1a"]
PART_NAMES["dbgf1_wtm"] = PART_NAMES["f1a"]
PART_NAMES["f3b"] = PART_NAMES["f3a"]
for _d in ("wtm", "wT8", "h0", "lgt"):
    PART_NAMES["dbgf0_" + _d] = PART_NAMES["f0a"]
PART_NAMES["f0b"] = PART_NAMES["f0a"]


def declare_dram(nc, sh, names):
    dram = {}
    for n in names:
        arr = sh[n]
        dram[n] = nc.dram_tensor(
            n, list(arr.shape),
            F32 if arr.dtype == np.float32 else BF16, kind="ExternalInput")
    dram["xin"] = nc.dram_tensor("xin", [D, T], F32, kind="ExternalInput")
    dram["xout"] = nc.dram_tensor("xout", [D, T], F32, kind="ExternalOutput")
    return dram


def l0a_handles(dram):
    return {"Wq": dram["L0a_wq"], "Wk": dram["L0a_wk"],
            "Wv": dram["L0a_wv"], "Wproj": dram["L0a_wproj"],
            "w1": dram["L0a_w1"], "w2": dram["L0a_w2"],
            "p_cols": dram["L0a_p_cols"], "p_colsb": dram["L0a_p_colsb"],
            "p_ab": dram["L0a_p_ab"], "p_bvrow": dram["L0a_p_bvrow"]}


def l2a_handles(dram):
    return {"WqT": dram["L2a_p_WqT"], "WqpT": dram["L2a_p_WqpT"],
            "WkT": dram["L2a_p_WkT"], "WkpT": dram["L2a_p_WkpT"],
            "WvT": dram["L2a_p_WvT"], "WoT": dram["L2a_p_WoT"],
            "p_cos": dram["L2a_p_cos"], "p_sin": dram["L2a_p_sin"],
            "p_bcols": dram["L2a_p_bcols"], "p_bocol": dram["L2a_p_bocol"],
            "p_cmask": dram["L2a_p_cmask"], "p_bvrow": dram["L2a_p_bvrow"]}


def l3a_handles(dram):
    return {"Wq": dram["L3a_Wq"], "Wk": dram["L3a_Wk"],
            "Wv": dram["L3a_Wv"], "Wo": dram["L3a_Wo"],
            "Wdown": dram["L3a_Wdown"], "Wupk": dram["L3a_Wupk"],
            "Wupv": dram["L3a_Wupv"], "decay": dram["L3a_p_decay"]}


def l2f_handles(dram):
    return {"WfcT": dram["L2f_p_WfcT"], "WprojT": dram["L2f_p_WprojT"]}


def l1a_handles(dram):
    return {"W1": dram["L1a_W1"], "W2": dram["L1a_W2"],
            "p_cols": dram["L1a_p_cols"], "p_b1h": dram["L1a_p_b1h"],
            "p_b1vrow": dram["L1a_p_b1vrow"]}


def l1f_handles(dram):
    return {"p_Wg": dram["L1f_p_Wg"], "Ws1": dram["L1f_Ws1"],
            "Ws3": dram["L1f_Ws3"], "Ws2": dram["L1f_Ws2"],
            "We1": dram["L1f_We1"], "We3": dram["L1f_We3"],
            "We2": dram["L1f_We2"]}


def l3f_handles(dram):
    return {"p_router": dram["L3f_p_router"], "p_gb": dram["L3f_p_gb"],
            "W1": dram["L3f_W1"], "W2": dram["L3f_W2"]}


def l0f_handles(dram):
    return {"p_Wr": dram["L0f_p_Wr"], "p_b1fm": dram["L0f_p_b1fm"],
            "W1": dram["L0f_W1"], "W2": dram["L0f_W2"],
            "b2": dram["L0f_b2"], "br": dram["L0f_br"]}


BASS_KW = {}


def build_group(parts, sh):
    nc = bass.Bass("TRN2", target_bir_lowering=False, **BASS_KW)
    names = set(["cst_b16", "cst_f32"])
    for p in parts:
        names |= PART_NAMES[p]
    dram = declare_dram(nc, sh, sorted(names))
    with ExitStack() as es:
        ctx = Ctx(nc, es)
        emit_load_globals(ctx, dram)
        first = True
        for p in parts:
            cv = ctx.carver()
            if not first:
                S_ = ctx.S
                bw = [(s, S_.now(s)) for s in ("mm", "dve", "act")]
                S_.add("sync", lambda e: None, waits=bw)
                S_.add("gpsimd", lambda e: None, waits=bw)
            first = False
            if p in ("f0a", "f0b"):
                ln = emit_spine_ln(ctx, cv)
                emit_f0(ctx, cv, l0f_handles(dram), ln, 0.5)
            elif p in ("f1a", "f1b"):
                ln = emit_spine_ln(ctx, cv)
                emit_f1(ctx, cv, l1f_handles(dram), ln, 0.5 / 3.0)
            elif p == "a0":
                emit_a0(ctx, cv, l0a_handles(dram))
            elif p == "a2":
                emit_a2(ctx, cv, l2a_handles(dram))
            elif p == "a3":
                emit_a3(ctx, cv, l3a_handles(dram))
            elif p in ("f2a", "f2b"):
                ln = emit_spine_ln(ctx, cv)
                emit_f2(ctx, cv, l2f_handles(dram), ln, 0.5)
            elif p == "a1":
                emit_a1(ctx, cv, l1a_handles(dram))
            elif p.startswith("dbgf1_"):
                ln = emit_spine_ln(ctx, cv)
                emit_f1(ctx, cv, l1f_handles(dram), ln, 0.5 / 3.0,
                        dbg=p.split("_")[1])
            elif p in ("f3a", "f3b"):
                scr = make_ln_scratch(cv)
                ln = emit_spine_ln(ctx, cv, scr=scr)
                emit_f3(ctx, cv, l3f_handles(dram), ln, 0.5, scr)
            elif p.startswith("dbgf0_"):
                ln = emit_spine_ln(ctx, cv)
                emit_f0(ctx, cv, l0f_handles(dram), ln, 0.5,
                        dbg=p.split("_")[1])
            elif p == "dbgln":
                ln = emit_spine_ln(ctx, cv)
                S = ctx.S
                for dc in range(DC):
                    S.add("scalar", lambda e, dc=dc: e.activation(
                        ctx.xc(dc), ctx.xnc(dc), AF.Copy),
                        waits=[("dve", ln)], inc="act")
            else:
                raise NotImplementedError(p)
            ctx.x_ready = None
        emit_store_spine(ctx, dram)
        ctx.S.flush()
    return nc, sorted(names)


PART_ORDER = ["f0a", "a0", "f0b", "f1a", "a1", "f1b",
              "f2a", "a2", "f2b", "f3a", "a3", "f3b"]
LAUNCH_GROUPS = [PART_ORDER]


def kernel(**inputs):
    sh = prep_shared(inputs)
    x = np.asarray(inputs["L0_attn_hidden_states"])
    shards = [np.ascontiguousarray(x[b].T) for b in range(NCORES)]
    for parts in LAUNCH_GROUPS:
        nc, names = build_group(parts, sh)
        in_maps = []
        for b in range(NCORES):
            m = {n: sh[n] for n in names}
            m["xin"] = shards[b]
            in_maps.append(m)
        res = run_bass_kernel_spmd(nc, in_maps, core_ids=list(range(NCORES)))
        shards = [np.asarray(res.results[b]["xout"]) for b in range(NCORES)]
    out = np.stack([s.T for s in shards]).astype(np.float32)
    return np.ascontiguousarray(out)
